# version 19
# baseline (speedup 1.0000x reference)
import numpy as np
from contextlib import ExitStack

import concourse.bass as bass
import concourse.mybir as mybir
from concourse.bass_utils import run_bass_kernel_spmd
from concourse.alu_op_type import AluOpType as OP

F32 = mybir.dt.float32
BF16 = mybir.dt.bfloat16
AF = mybir.ActivationFunctionType

D = 1024
S = 4096
T = 512
NT = S // T
KC = D // 128
FS = 256
DFF = 4096
NE = 8
GUF = 2048
EPS = 1e-5
RSQ2 = float(1.0 / np.sqrt(2.0))

STOP_STAGE = "full"
DBG = False

ENGS = ("tensor", "vector", "scalar", "gpsimd", "sync")


def build(stop_stage="full"):
    nc = bass.Bass("TRN2", target_bir_lowering=False)

    xT = nc.dram_tensor("xT", [D, S], F32, kind="ExternalInput").ap()
    wtoT = nc.dram_tensor("wtoT", [D, 2 * FS], F32, kind="ExternalInput").ap()
    wfromT = nc.dram_tensor("wfromT", [2 * FS, D], F32, kind="ExternalInput").ap()
    w1T = nc.dram_tensor("w1T", [D, DFF], F32, kind="ExternalInput").ap()
    w2T = nc.dram_tensor("w2T", [DFF, D], F32, kind="ExternalInput").ap()
    rtT = nc.dram_tensor("rtT", [D, NE], F32, kind="ExternalInput").ap()
    guT = nc.dram_tensor("guT", [NE, D, GUF], F32, kind="ExternalInput").ap()
    dpT = nc.dram_tensor("dpT", [NE, D, D], F32, kind="ExternalInput").ap()
    ln1g = nc.dram_tensor("ln1g", [D], F32, kind="ExternalInput").ap()
    ln1b = nc.dram_tensor("ln1b", [D], F32, kind="ExternalInput").ap()
    ln3g = nc.dram_tensor("ln3g", [D], F32, kind="ExternalInput").ap()
    ln3b = nc.dram_tensor("ln3b", [D], F32, kind="ExternalInput").ap()
    bf1 = nc.dram_tensor("bf1", [DFF], F32, kind="ExternalInput").ap()
    bf2 = nc.dram_tensor("bf2", [D], F32, kind="ExternalInput").ap()
    ldec = nc.dram_tensor("ldec", [FS], F32, kind="ExternalInput").ap()
    freq = nc.dram_tensor("freq", [FS], F32, kind="ExternalInput").ap()
    eye_in = nc.dram_tensor("eye_in", [128, 128], F32, kind="ExternalInput").ap()
    outT = nc.dram_tensor("outT", [D, S], F32, kind="ExternalOutput").ap()

    scr_w1 = nc.dram_tensor("scr_w1", [8, 128, 4096], BF16, kind="Internal").ap()
    scr_w2 = nc.dram_tensor("scr_w2", [8, 128, 4096], BF16, kind="Internal").ap()
    scr_gu = nc.dram_tensor("scr_gu", [NE, 4, 128, 4096], BF16, kind="Internal").ap()
    scr_dp = nc.dram_tensor("scr_dp", [NE, 2, 128, 4096], BF16, kind="Internal").ap()

    es = ExitStack()

    def sb(name, shape, dt=F32):
        return es.enter_context(nc.sbuf_tensor(name, shape, dt))

    def psum(name, shape, dt=F32):
        return es.enter_context(nc.psum_tensor(name, shape, dt))

    with es:
        X0 = sb("X0", [128, KC * T])
        X2 = sb("X2", [128, KC * T])
        ACC = sb("ACC", [128, KC * T])
        BFINS = [sb("BFIN0", [128, KC * T], BF16), sb("BFIN1", [128, KC * T], BF16)]
        VR = sb("VR", [128, 2 * T]);  VI = sb("VI", [128, 2 * T])
        YR = sb("YR", [128, T]);  YI = sb("YI", [128, T])
        ZS = sb("ZS", [128, T])
        ZB = sb("ZB", [128, 4 * T], BF16)
        DFU = sb("DFU", [128, 2 * T])
        TBW = 514
        CF = sb("CF", [128, 2 * TBW])
        SF = sb("SF", [128, 2 * TBW])
        STR = sb("STR", [128, 2]); STI = sb("STI", [128, 2])
        CM1 = sb("CM1", [128, 1]); SM1 = sb("SM1", [128, 1])
        G = sb("G", [128, 32 * T], BF16)
        SG = sb("SG", [128, 8 * T], BF16)
        MEANT = sb("MEANT", [128, T])
        RSTD = sb("RSTD", [128, T])
        NEGH = sb("NEGH", [128, 1])
        UPE = sb("UPE", [128, 2 * T])
        PRD = sb("PRD", [128, 8 * T], BF16)
        CBE = sb("CBE", [128, T])
        WTO = sb("WTO", [128, KC * 512], BF16)
        WFR = sb("WFR", [128, 4 * 1024], BF16)
        RT = sb("RT", [128, KC * 9], BF16)
        RING = sb("RING", [128, 4 * 4096], BF16)
        ONESB = sb("ONESB", [128, 128], BF16)
        ONER = sb("ONER", [1, T], BF16)
        EYE = sb("EYE", [128, 128])
        EH = sb("EH", [8, NE * 128], BF16)
        B2R = sb("B2R", [1, D], BF16)
        G1V = sb("G1V", [128, KC]); B1V = sb("B1V", [128, KC])
        G3V = sb("G3V", [128, KC]); B3V = sb("B3V", [128, KC])
        BHV = sb("BHV", [128, 32])
        LDV = sb("LDV", [128, 2]); FRV = sb("FRV", [128, 2])
        DEC = sb("DEC", [128, 2]); PI2 = sb("PI2", [128, 1])
        LGS = sb("LGS", [128, 32])
        SRB = sb("SRB", [128, 8])
        CMB = sb("CMB", [128, 32])
        CTE = sb("CTE", [8, T], BF16)
        MX4 = ZS[:, 0:32]; WW4A = ZS[:, 32:36]; WW4B = ZS[:, 36:40]

        PB = [psum("PB%d" % i, [128, 512]) for i in range(8)]

        sems = {}
        for name in ("SX", "R0", "R1", "R2", "R3", "SR", "SO", "T", "V", "S", "G"):
            sems[name] = es.enter_context(nc.semaphore("sem_" + name))
        STEP = {"SX": 16, "R0": 16, "R1": 16, "R2": 16, "R3": 16, "SR": 16, "SO": 16,
                "T": 1, "V": 1, "S": 1, "G": 1}

        q = {k: [] for k in ENGS}
        cnt = {k: 0 for k in sems}

        def w(eng, ms):
            if ms is None:
                return
            if isinstance(ms, list):
                for m in ms:
                    w(eng, m)
                return
            sk, v = ms
            q[eng].append(lambda E, s=sems[sk], vv=v: E.wait_ge(s, vv))

        DRAIN_ENGS = ("vector",)

        def op(eng, fn):
            q[eng].append(fn)
            if eng in DRAIN_ENGS:
                q[eng].append(lambda E: E.drain())

        def opi(eng, sk, fn):
            cnt[sk] += STEP[sk]
            v = cnt[sk]
            q[eng].append(lambda E, fn=fn, s=sems[sk], st=STEP[sk]: fn(E).then_inc(s, st))
            if eng in DRAIN_ENGS:
                q[eng].append(lambda E: E.drain())
            return (sk, v)

        def opd(eng, sk, fn):
            cnt[sk] += STEP[sk]
            v = cnt[sk]
            q[eng].append(lambda E, fn=fn, s=sems[sk], st=STEP[sk]: fn(E).then_inc(s, st))
            return (sk, v)

        def vmark():
            return ("V", cnt["V"])

        def xc(buf, k, width=T):
            return buf[:, k * width:(k + 1) * width]

        bank_free = [None] * 8
        bank_ready = [None] * 8

        NRS = 4
        ring_free = [None] * NRS
        ring_idx = [0]

        def ring_slot():
            j = ring_idx[0] % NRS
            ring_idx[0] += 1
            ms = ring_free[j]
            ring_free[j] = None
            return j, RING[:, j * 4096:(j + 1) * 4096], ms

        def ring_set_free(j, ms):
            if isinstance(ring_free[j], list):
                ring_free[j] = ring_free[j] + [ms]
            else:
                ring_free[j] = ms

        def slot_dma(j, fn):
            return opd("sync", "R%d" % j, fn)

        op("vector", lambda E: E.memset(ONESB[:], 1.0))
        op("vector", lambda E: E.memset(ONER[:], 1.0))
        op("vector", lambda E: E.memset(PI2[:], np.pi / 2))
        op("vector", lambda E: E.memset(STR[:], 0.0))
        op("vector", lambda E: E.memset(STI[:], 0.0))
        op("vector", lambda E: E.memset(NEGH[:], -0.5))
        op("gpsimd", lambda E: E.iota(out=EH[:], pattern=[[-1, NE], [0, 128]], base=0, channel_multiplier=1, allow_small_or_imprecise_dtypes=True))
        op("gpsimd", lambda E: E.drain())
        ms_eh = opi("gpsimd", "G", lambda E: E.tensor_scalar(out=EH[:], in0=EH[:], scalar1=0.0, scalar2=None, op0=OP.is_equal))

        jb, slotb, msfb = ring_slot()
        w("sync", msfb)
        ms_b2 = slot_dma(jb, lambda E, d=slotb.bitcast(F32)[0:1, 0:D],
                         s=bf2.rearrange("(a d) -> a d", a=1): E.dma_start(out=d, in_=s))
        w("vector", ms_b2)
        ms_b2c = opi("vector", "V", lambda E, s=slotb.bitcast(F32)[0:1, 0:D]: E.tensor_copy(out=B2R[:], in_=s))
        ring_set_free(jb, ms_b2c)

        small_loads = [
            (EYE[:], eye_in[:]),
            (G1V[:], ln1g.rearrange("(k p) -> p k", p=128)),
            (B1V[:], ln1b.rearrange("(k p) -> p k", p=128)),
            (G3V[:], ln3g.rearrange("(k p) -> p k", p=128)),
            (B3V[:], ln3b.rearrange("(k p) -> p k", p=128)),
            (BHV[:], bf1.rearrange("(k p) -> p k", p=128)),
            (LDV[:], ldec.rearrange("(k p) -> p k", p=128)),
            (FRV[:], freq.rearrange("(k p) -> p k", p=128)),
        ]
        ms_sl = None
        for dst, src in small_loads:
            ms_sl = opd("scalar", "SR", lambda E, d=dst, s=src: E.dma_start(out=d, in_=s, allow_slow_non_contiguous=True))

        ms_bh = ms_b2c

        w("scalar", ms_sl)
        ms_dec = opi("scalar", "S", lambda E: E.activation(out=DEC[:], in_=LDV[:], func=AF.Sigmoid))
        ms_th = opi("scalar", "S", lambda E: E.activation(out=FRV[:], in_=FRV[:], func=AF.Tanh))
        op("scalar", lambda E: E.drain())

        def cfc(ch, a, b):
            return CF[:, ch * TBW + a: ch * TBW + b]

        def sfc(ch, a, b):
            return SF[:, ch * TBW + a: ch * TBW + b]

        w("scalar", [ms_th, ms_bh])
        ms_seed = None
        for ch in range(2):
            opi("scalar", "S", lambda E, ch=ch: E.activation(out=cfc(ch, 1, 2), in_=FRV[:, ch:ch + 1], func=AF.Sin, scale=0.1, bias=PI2[:]))
            ms_seed = opi("scalar", "S", lambda E, ch=ch: E.activation(out=sfc(ch, 1, 2), in_=FRV[:, ch:ch + 1], func=AF.Sin, scale=0.1))
        for ch in range(2):
            op("vector", lambda E, ch=ch: E.memset(cfc(ch, 0, 1), 1.0))
            op("vector", lambda E, ch=ch: E.memset(sfc(ch, 0, 1), 0.0))
        w("vector", ms_seed)
        ms_tab = None
        for ch in range(2):
            K = 2
            while K < T + 1:
                n = min(K - 1, T + 1 - K)
                opi("vector", "V", lambda E, ch=ch, K=K: E.tensor_copy(out=CM1[:], in_=cfc(ch, K - 1, K)))
                opi("vector", "V", lambda E, ch=ch, K=K: E.tensor_copy(out=SM1[:], in_=sfc(ch, K - 1, K)))
                op("vector", lambda E: E.drain())
                opi("vector", "V", lambda E, ch=ch, K=K, n=n: E.tensor_scalar(out=cfc(ch, K, K + n), in0=cfc(ch, 1, 1 + n), scalar1=CM1[:], scalar2=None, op0=OP.mult))
                opi("vector", "V", lambda E, ch=ch, n=n: E.tensor_scalar(out=YR[:, 0:n], in0=sfc(ch, 1, 1 + n), scalar1=SM1[:], scalar2=None, op0=OP.mult))
                opi("vector", "V", lambda E, ch=ch, K=K, n=n: E.tensor_tensor(out=cfc(ch, K, K + n), in0=cfc(ch, K, K + n), in1=YR[:, 0:n], op=OP.subtract))
                opi("vector", "V", lambda E, ch=ch, K=K, n=n: E.tensor_scalar(out=sfc(ch, K, K + n), in0=sfc(ch, 1, 1 + n), scalar1=CM1[:], scalar2=None, op0=OP.mult))
                opi("vector", "V", lambda E, ch=ch, n=n: E.tensor_scalar(out=YR[:, 0:n], in0=cfc(ch, 1, 1 + n), scalar1=SM1[:], scalar2=None, op0=OP.mult))
                ms_tab = opi("vector", "V", lambda E, ch=ch, K=K, n=n: E.tensor_tensor(out=sfc(ch, K, K + n), in0=sfc(ch, K, K + n), in1=YR[:, 0:n], op=OP.add))
                K += n
        w("vector", ms_dec)
        op("vector", lambda E: E.memset(DFU[:], 1.0))
        ms_dfu = None
        for ch in range(2):
            ms_dfu = opi("vector", "V", lambda E, ch=ch: E.tensor_scalar(out=xc(DFU, ch), in0=xc(DFU, ch), scalar1=DEC[:, ch:ch + 1], scalar2=None, op0=OP.mult))

        def prepass_unit(srcs, dst_sb=None, dst_dram=None, scale=None, ncols=2048):
            halves = []
            for hv in range(len(srcs)):
                jj, slot, msf = ring_slot()
                w("sync", msf)
                g = srcs[hv].shape[0]
                gc_ = ncols // g
                dview = slot.bitcast(F32)[:, 0:ncols].rearrange("p (k f) -> p k f", k=g)
                sview = srcs[hv].rearrange("k p f -> p k f")
                ms_rd = slot_dma(jj, lambda E, d=dview, s=sview: E.dma_start(out=d, in_=s))
                halves.append((jj, slot, ms_rd))
            CENG = (("vector", "V"), ("scalar", "S"))
            if dst_sb is not None:
                ms_cs = []
                for hv, (jj, slot, ms_rd) in enumerate(halves):
                    eng, sk = CENG[hv % 2]
                    w(eng, ms_rd)
                    dstap = dst_sb[:, hv * ncols:(hv + 1) * ncols]
                    if eng == "scalar":
                        ms_c = opi(eng, sk, lambda E, d=dstap, s=slot.bitcast(F32)[:, 0:ncols], sc=(scale or 1.0): E.activation(out=d, in_=s, func=AF.Copy, scale=sc))
                    elif scale is None:
                        ms_c = opi(eng, sk, lambda E, d=dstap, s=slot.bitcast(F32)[:, 0:ncols]: E.tensor_copy(out=d, in_=s))
                    else:
                        ms_c = opi(eng, sk, lambda E, d=dstap, s=slot.bitcast(F32)[:, 0:ncols], sc=scale: E.tensor_scalar(out=d, in0=s, scalar1=sc, scalar2=None, op0=OP.mult))
                    ring_set_free(jj, ms_c)
                    ms_cs.append(ms_c)
                return ms_cs, None
            jo, oslot, msfo = ring_slot()
            ms_cs = []
            for hv, (jj, slot, ms_rd) in enumerate(halves):
                eng, sk = CENG[hv % 2]
                w(eng, ms_rd)
                if hv == 0:
                    w(eng, msfo)
                else:
                    w(eng, msfo)
                dstap = oslot[:, hv * ncols:(hv + 1) * ncols]
                if eng == "scalar":
                    ms_c = opi(eng, sk, lambda E, d=dstap, s=slot.bitcast(F32)[:, 0:ncols]: E.activation(out=d, in_=s, func=AF.Copy))
                else:
                    ms_c = opi(eng, sk, lambda E, d=dstap, s=slot.bitcast(F32)[:, 0:ncols]: E.tensor_copy(out=d, in_=s))
                ring_set_free(jj, ms_c)
                ms_cs.append(ms_c)
            w("sync", ms_cs)
            ms_wr = slot_dma(jo, lambda E, d=dst_dram, s=oslot[:]: E.dma_start(out=d, in_=s))
            return ms_cs, ms_wr, jo, oslot

        wtoR = wtoT.rearrange("(k p) f -> k p f", p=128)
        wfromR = wfromT.rearrange("(c p) d -> c p d", p=128)
        w1R = w1T.rearrange("(k p) f -> k p f", p=128)
        w2R = w2T.rearrange("(k p) d -> k p d", p=128)
        guR = guT.rearrange("e (k p) f -> e k p f", p=128)
        dpR = dpT.rearrange("e (k p) d -> e k p d", p=128)
        rtR = rtT.rearrange("(k p) e -> k p e", p=128)

        xTr = xT.rearrange("(k p) t -> k p t", p=128)
        outTr = outT.rearrange("(k p) t -> k p t", p=128)
        x_ms = [None] * (NT + 1)

        def emit_xload(i, extra=None):
            t0 = i * T
            w("sync", extra)
            ms = None
            for k in range(KC):
                ms = opd("sync", "SX", lambda E, k=k, t0=t0: E.dma_start(out=xc(X0, k), in_=xTr[k, :, t0:t0 + T]))
            x_ms[i] = ms

        emit_xload(0)

        ms_wto, _ = prepass_unit(
            [wtoR[0:4], wtoR[4:8]],
            dst_sb=WTO[:], scale=0.1)
        ms_wfr, _ = prepass_unit(
            [wfromR[0:2], wfromR[2:4]],
            dst_sb=WFR[:])
        jr, slot_r, msf = ring_slot()
        w("sync", msf)
        ms_rt_rd = None
        for kk in range(KC):
            ms_rt_rd = slot_dma(jr, lambda E, d=slot_r.bitcast(F32)[:, kk * 8:(kk + 1) * 8], s=rtR[kk]: E.dma_start(out=d, in_=s))
        w("gpsimd", ms_rt_rd)
        ms_rt = None
        for kk in range(KC):
            ms_rt = opi("gpsimd", "G", lambda E, kk=kk, s=slot_r.bitcast(F32)[:, kk * 8:(kk + 1) * 8]: E.tensor_copy(out=RT[:, kk * 9:kk * 9 + 8], in_=s))
        ring_set_free(jr, ms_rt)
        ms_rto = opi("vector", "V", lambda E: E.memset(RT[:].rearrange("p (k e) -> p k e", k=KC)[:, :, 8], 1.0))
        w("tensor", [ms_rt, ms_b2c, ms_rto])
        ms_srb_mm = None
        for kk in range(KC):
            ms_srb_mm = opi("tensor", "T", lambda E, kk=kk: E.matmul(PB[6][:, 0:8], ONESB[:], RT[:, kk * 9:kk * 9 + 8], start=(kk == 0), stop=(kk == KC - 1)))
        w("scalar", ms_srb_mm)
        ms_srb = opi("scalar", "S", lambda E: E.activation(out=SRB[:], in_=PB[6][:, 0:8], func=AF.Copy))
        bank_free[6] = ms_srb

        scr_ready = {}

        def _scr_srcs(key):
            if key[0] == "w1":
                fr = key[1]
                return ([w1R[0:4, :, fr * 512:(fr + 1) * 512],
                         w1R[4:8, :, fr * 512:(fr + 1) * 512]],
                        scr_w1[fr])
            if key[0] == "w2":
                fr, kr = key[1] // 4, key[1] % 4
                return ([w2R[kr * 8:kr * 8 + 4, :, fr * 512:(fr + 1) * 512],
                         w2R[kr * 8 + 4:kr * 8 + 8, :, fr * 512:(fr + 1) * 512]],
                        scr_w2[key[1]])
            if key[0] == "gu":
                _, e, fr = key
                return ([guR[e, 0:4, :, fr * 512:(fr + 1) * 512],
                         guR[e, 4:8, :, fr * 512:(fr + 1) * 512]],
                        scr_gu[e, fr])
            _, e, fr = key
            return ([dpR[e, 0:4, :, fr * 512:(fr + 1) * 512],
                     dpR[e, 4:8, :, fr * 512:(fr + 1) * 512]],
                    scr_dp[e, fr])




        def emit_slab_load(dram_ap, key):
            if key not in scr_ready:
                srcs, dram = _scr_srcs(key)
                ms_cs, mw, jo, oslot = prepass_unit(srcs, dst_dram=dram)
                scr_ready[key] = mw
                ring_free[jo] = [mw]
                return jo, oslot, ms_cs
            jj, slot, msf = ring_slot()
            w("sync", msf)
            ms = slot_dma(jj, lambda E, d=slot, s=dram_ap: E.dma_start(out=d, in_=s))
            return jj, slot, ms

        bfin_user = [None, None]

        def emit_ln(src, bfin_p, gv=None, bv=None, dstf=None, out_bf=True,
                    src_ready=None, extra_swait=None, extra_twait=None, extra_vwait=None,
                    twopass=False, apply_gp=False):
            BFIN = BFINS[bfin_p]
            per_chunk = isinstance(src_ready, list) and len(src_ready) == KC
            if not per_chunk:
                w("scalar", src_ready)
            w("scalar", extra_swait)
            w("scalar", bfin_user[bfin_p])
            cast_ms = [None] * KC
            sq_ms = [None] * KC
            mmS_ms = [None] * KC
            SQB = BFINS[1 - bfin_p] if twopass else BFIN
            sq_p = (1 - bfin_p) if twopass else bfin_p
            if twopass:
                w("scalar", bfin_user[sq_p])
            for k in range(KC):
                if per_chunk:
                    w("scalar", src_ready[k])
                cast_ms[k] = opi("scalar", "S", lambda E, k=k: E.activation(out=xc(BFIN, k), in_=src(k), func=AF.Copy))
            w("tensor", bank_free[0])
            w("tensor", extra_twait)
            w("tensor", cast_ms[KC - 1])
            for k in range(KC):
                mmS_ms[k] = opi("tensor", "T", lambda E, k=k: E.matmul(PB[0][:], ONESB[:], xc(BFIN, k), start=(k == 0), stop=(k == KC - 1)))
            for k in range(KC):
                if not twopass:
                    w("scalar", mmS_ms[k])
                sq_ms[k] = opi("scalar", "S", lambda E, k=k: E.activation(out=xc(SQB, k), in_=src(k), func=AF.Square))
            w("tensor", bank_free[1])
            mmQ_last = None
            for k in range(KC):
                w("tensor", sq_ms[k])
                mmQ_last = opi("tensor", "T", lambda E, k=k: E.matmul(PB[1][:], ONESB[:], xc(SQB, k), start=(k == 0), stop=(k == KC - 1)))
            bank_ready[0] = mmS_ms[KC - 1]
            bank_ready[1] = mmQ_last
            bfin_user[sq_p] = mmQ_last
            if twopass:
                bfin_user[bfin_p] = None
            w("vector", bank_ready[0])
            w("vector", [st.get("p2_done"), st.get("pl2_done")])
            opi("vector", "V", lambda E: E.tensor_scalar(out=MEANT[:], in0=PB[0][:], scalar1=float(1.0 / D), scalar2=None, op0=OP.mult))
            opi("vector", "V", lambda E: E.tensor_tensor(out=ZS[:, 0:T], in0=MEANT[:], in1=MEANT[:], op=OP.mult))
            w("vector", bank_ready[1])
            opi("vector", "V", lambda E: E.scalar_tensor_tensor(out=PB[1][:], in0=PB[1][:], scalar=float(1.0 / D), in1=ZS[:, 0:T], op0=OP.mult, op1=OP.subtract))
            ms_veps = opi("vector", "V", lambda E: E.tensor_scalar(out=PB[1][:], in0=PB[1][:], scalar1=float(EPS), scalar2=None, op0=OP.add))
            w("scalar", ms_veps)
            ms_sqr = opi("scalar", "S", lambda E: E.activation(out=PB[1][:], in_=PB[1][:], func=AF.Sqrt))
            w("vector", ms_sqr)
            ms_rec = opi("vector", "V", lambda E: E.reciprocal(out=RSTD[:], in_=PB[1][:]))
            bank_free[1] = ms_rec
            ms_pow = ms_rec
            bf_ms = [None] * KC
            f_ms = [None] * KC
            if twopass:
                st["p1_stats"] = ms_pow

                def pass1():
                    for k in range(KC):
                        opi("vector", "V", lambda E, k=k: E.tensor_tensor(out=ZS[:, 0:T], in0=src(k), in1=MEANT[:], op=OP.subtract))
                        bf_ms[k] = opi("vector", "V", lambda E, k=k: E.tensor_tensor(out=xc(BFIN, k), in0=ZS[:, 0:T], in1=RSTD[:], op=OP.mult))
                    bank_free[0] = bf_ms[KC - 1]
                    bank_free[1] = bf_ms[KC - 1]
                st["p1_pass1"] = pass1

                def p2_step(k):
                    def go():
                        if k == 0:
                            w("gpsimd", [extra_vwait, bf_ms[KC - 1]])
                        ms_sub = opi("gpsimd", "G", lambda E, k=k: E.tensor_tensor(out=dstf(k), in0=src(k), in1=MEANT[:], op=OP.subtract))
                        w("gpsimd", ms_sub)
                        f_ms[k] = opi("gpsimd", "G", lambda E, k=k: E.tensor_tensor(out=dstf(k), in0=dstf(k), in1=RSTD[:], op=OP.mult))
                        st["p2f_%d" % k] = f_ms[k]
                        if k == KC - 1:
                            st["p2_done"] = f_ms[k]
                    return go
                st["p1_pass2_steps"] = [p2_step(k) for k in range(KC)]
                st["p1_sq"] = sq_ms
                return cast_ms, None, sq_ms[KC - 1], bf_ms, f_ms
            if apply_gp:
                w("gpsimd", [ms_rec, extra_vwait])
                for k in range(KC):
                    m1 = opi("gpsimd", "G", lambda E, k=k: E.tensor_tensor(out=dstf(k), in0=src(k), in1=MEANT[:], op=OP.subtract))
                    w("gpsimd", m1)
                    f_ms[k] = opi("gpsimd", "G", lambda E, k=k: E.tensor_tensor(out=dstf(k), in0=dstf(k), in1=RSTD[:], op=OP.mult))
                st["pl2_done"] = f_ms[KC - 1]
                bank_free[0] = f_ms[KC - 1]
                bank_free[1] = f_ms[KC - 1]
                return None, f_ms[KC - 1], sq_ms[KC - 1], bf_ms, f_ms
            w("vector", extra_vwait)
            for k in range(KC):
                opi("vector", "V", lambda E, k=k: E.tensor_tensor(out=ZS[:, 0:T], in0=src(k), in1=MEANT[:], op=OP.subtract))
                if dstf is not None:
                    f_ms[k] = opi("vector", "V", lambda E, k=k: E.tensor_tensor(out=dstf(k), in0=ZS[:, 0:T], in1=RSTD[:], op=OP.mult))
                    if out_bf:
                        bf_ms[k] = opi("vector", "V", lambda E, k=k: E.tensor_copy(out=xc(BFIN, k), in_=dstf(k)))
                else:
                    opi("vector", "V", lambda E, k=k: E.tensor_tensor(out=ZS[:, 0:T], in0=ZS[:, 0:T], in1=RSTD[:], op=OP.mult))
                    bf_ms[k] = opi("vector", "V", lambda E, k=k: E.tensor_scalar(out=xc(BFIN, k), in0=ZS[:, 0:T], scalar1=gv[:, k:k + 1], scalar2=bv[:, k:k + 1], op0=OP.mult, op1=OP.add))
            last_bf = bf_ms[KC - 1]
            last_f = f_ms[KC - 1]
            mfin = last_f if (dstf is not None and not out_bf) else last_bf
            bank_free[0] = mfin
            bank_free[1] = mfin
            return last_bf, last_f, sq_ms[KC - 1], bf_ms, f_ms

        st = {}

        def make_chain(j):
            t0 = j * T
            BFIN = BFINS[j % 2]
            st.pop("ln1_bf", None)

            def spec_mms(fc, wait_s, wait_v, percol=None):
                w("tensor", st.get("b7free"))
                w("tensor", wait_s)
                w("tensor", wait_v)
                if j == 0:
                    w("tensor", ms_wto)
                mmm = None
                for k in range(KC):
                    if percol is not None:
                        w("tensor", percol[k])
                    mmm = opi("tensor", "T", lambda E, k=k, fc=fc: E.matmul(
                        PB[7][:], WTO[:, k * 512 + fc * 128: k * 512 + (fc + 1) * 128], xc(BFIN, k),
                        start=(k == 0), stop=(k == KC - 1)))
                return mmm

            def prep(ch, ms_ur_evac, ms_ui_mm):
                if j == 0:
                    w("vector", [ms_tab, ms_dfu])
                w("vector", [ms_ur_evac, ms_ui_mm])
                opi("vector", "V", lambda E, ch=ch: E.tensor_tensor(out=xc(VR, ch), in0=ZS[:, 0:T], in1=cfc(ch, 0, T), op=OP.mult))
                opi("vector", "V", lambda E, ch=ch: E.tensor_tensor(out=YR[:, 0:T], in0=PB[7][:], in1=sfc(ch, 0, T), op=OP.mult))
                opi("vector", "V", lambda E, ch=ch: E.tensor_tensor(out=xc(VR, ch), in0=xc(VR, ch), in1=YR[:, 0:T], op=OP.add))
                opi("vector", "V", lambda E, ch=ch: E.tensor_tensor(out=xc(VI, ch), in0=PB[7][:], in1=cfc(ch, 0, T), op=OP.mult))
                opi("vector", "V", lambda E, ch=ch: E.tensor_tensor(out=YI[:, 0:T], in0=ZS[:, 0:T], in1=sfc(ch, 0, T), op=OP.mult))
                return opi("vector", "V", lambda E, ch=ch: E.tensor_tensor(out=xc(VI, ch), in0=xc(VI, ch), in1=YI[:, 0:T], op=OP.subtract))

            steps = []

            def s_ln1a():
                w("scalar", x_ms[j])
                w("scalar", bfin_user[j % 2])
                cast_ms = [None] * KC
                sq_ms = [None] * KC
                mmS_ms = [None] * KC
                for k in range(KC):
                    cast_ms[k] = opi("scalar", "S", lambda E, k=k: E.activation(out=xc(BFIN, k), in_=xc(X0, k), func=AF.Copy))
                w("tensor", bank_free[0])
                if j == 0:
                    w("tensor", ms_bh)
                w("tensor", cast_ms[KC - 1])
                for k in range(KC):
                    mmS_ms[k] = opi("tensor", "T", lambda E, k=k: E.matmul(PB[0][:], ONESB[:], xc(BFIN, k), start=(k == 0), stop=(k == KC - 1)))
                for k in range(KC):
                    w("scalar", mmS_ms[k])
                    sq_ms[k] = opi("scalar", "S", lambda E, k=k: E.activation(out=xc(BFIN, k), in_=xc(X0, k), func=AF.Square))
                w("tensor", bank_free[1])
                mmQ_last = None
                for k in range(KC):
                    w("tensor", sq_ms[k])
                    mmQ_last = opi("tensor", "T", lambda E, k=k: E.matmul(PB[1][:], ONESB[:], xc(BFIN, k), start=(k == 0), stop=(k == KC - 1)))
                bank_ready[0] = mmS_ms[KC - 1]
                bank_ready[1] = mmQ_last
                bfin_user[j % 2] = mmQ_last
            steps.append(s_ln1a)

            def s_ln1b():
                w("vector", [bank_ready[0], st.get("p2_done"), st.get("pl2_done")])
                opi("vector", "V", lambda E: E.tensor_scalar(out=MEANT[:], in0=PB[0][:], scalar1=float(1.0 / D), scalar2=None, op0=OP.mult))
                opi("vector", "V", lambda E: E.tensor_tensor(out=ZS[:, 0:T], in0=MEANT[:], in1=MEANT[:], op=OP.mult))
                w("vector", bank_ready[1])
                w("vector", st.get("p2_done"))
                opi("vector", "V", lambda E: E.scalar_tensor_tensor(out=PB[1][:], in0=PB[1][:], scalar=float(1.0 / D), in1=ZS[:, 0:T], op0=OP.mult, op1=OP.subtract))
                ms_veps = opi("vector", "V", lambda E: E.tensor_scalar(out=PB[1][:], in0=PB[1][:], scalar1=float(EPS), scalar2=None, op0=OP.add))
                w("scalar", ms_veps)
                ms_sqr = opi("scalar", "S", lambda E: E.activation(out=PB[1][:], in_=PB[1][:], func=AF.Sqrt))
                w("vector", ms_sqr)
                ms_rec = opi("vector", "V", lambda E: E.reciprocal(out=RSTD[:], in_=PB[1][:]))
                bank_free[1] = ms_rec
                ms_pow = ms_rec
            steps.append(s_ln1b)

            def s_ln1c(ks):
                bf_ms = st.setdefault("ln1_bf", [None] * KC)
                for k in ks:
                    opi("vector", "V", lambda E, k=k: E.tensor_tensor(out=ZS[:, 0:T], in0=xc(X0, k), in1=MEANT[:], op=OP.subtract))
                    opi("vector", "V", lambda E, k=k: E.tensor_tensor(out=ZS[:, 0:T], in0=ZS[:, 0:T], in1=RSTD[:], op=OP.mult))
                    bf_ms[k] = opi("vector", "V", lambda E, k=k: E.tensor_scalar(out=xc(BFIN, k), in0=ZS[:, 0:T], scalar1=G1V[:, k:k + 1], scalar2=B1V[:, k:k + 1], op0=OP.mult, op1=OP.add))
                if ks[-1] == KC - 1:
                    st["ln1_apply"] = bf_ms[KC - 1]
                    bank_free[0] = bf_ms[KC - 1]
                    bank_free[1] = bf_ms[KC - 1]
            steps.append(lambda: s_ln1c((0, 1, 2)))
            steps.append(lambda: s_ln1c((3, 4, 5)))
            steps.append(lambda: s_ln1c((6, 7)))

            def s_spec0():
                mm0 = spec_mms(0, None, None, percol=st["ln1_bf"])
                w("scalar", [mm0, st["ln1_apply"]])
                st["ev0"] = opi("scalar", "S", lambda E: E.activation(out=ZS[:, 0:T], in_=PB[7][:], func=AF.Copy))
                st["b7free"] = st["ev0"]
            steps.append(s_spec0)

            def s_spec2():
                st["mm2"] = spec_mms(2, st["ev0"], None)
            steps.append(s_spec2)

            def s_prep0():
                st["prep0"] = prep(0, st["ev0"], st["mm2"])
                st["b7free"] = st["prep0"]
            steps.append(s_prep0)

            def s_spec1():
                mm1 = spec_mms(1, None, st["prep0"])
                w("scalar", [mm1, st["prep0"]])
                st["ev1"] = opi("scalar", "S", lambda E: E.activation(out=ZS[:, 0:T], in_=PB[7][:], func=AF.Copy))
                st["b7free"] = st["ev1"]
            steps.append(s_spec1)

            def s_spec3():
                st["mm3"] = spec_mms(3, st["ev1"], None)
            steps.append(s_spec3)

            def s_prep1():
                st["prep1"] = prep(1, st["ev1"], st["mm3"])
                st["b7free"] = st["prep1"]
            steps.append(s_prep1)

            def s_sp(ch):
                opi("vector", "V", lambda E, ch=ch: E.tensor_tensor_scan(out=YR[:, 0:T], data0=xc(DFU, ch), data1=xc(VR, ch), initial=STR[:, ch:ch + 1], op0=OP.mult, op1=OP.add))
                opi("vector", "V", lambda E, ch=ch: E.tensor_tensor_scan(out=YI[:, 0:T], data0=xc(DFU, ch), data1=xc(VI, ch), initial=STI[:, ch:ch + 1], op0=OP.mult, op1=OP.add))
                opi("vector", "V", lambda E, ch=ch: E.tensor_tensor(out=xc(VR, ch), in0=YR[:, 0:T], in1=cfc(ch, 1, T + 1), op=OP.mult))
                opi("vector", "V", lambda E, ch=ch: E.tensor_tensor(out=xc(VI, ch), in0=YI[:, 0:T], in1=sfc(ch, 1, T + 1), op=OP.mult))
                opi("vector", "V", lambda E, ch=ch: E.tensor_tensor(out=xc(VR, ch), in0=xc(VR, ch), in1=xc(VI, ch), op=OP.subtract))
                opi("vector", "V", lambda E, ch=ch: E.tensor_scalar(out=xc(ZB, ch), in0=xc(VR, ch), scalar1=10.0, scalar2=-10.0, op0=OP.min, op1=OP.max))
                opi("vector", "V", lambda E, ch=ch: E.tensor_scalar(out=STR[:, ch:ch + 1], in0=VR[:, ch * T + T - 1: ch * T + T], scalar1=10.0, scalar2=-10.0, op0=OP.min, op1=OP.max))
                opi("vector", "V", lambda E, ch=ch: E.tensor_tensor(out=xc(VI, ch), in0=YI[:, 0:T], in1=cfc(ch, 1, T + 1), op=OP.mult))
                opi("vector", "V", lambda E, ch=ch: E.tensor_tensor(out=YR[:, 0:T], in0=YR[:, 0:T], in1=sfc(ch, 1, T + 1), op=OP.mult))
                opi("vector", "V", lambda E, ch=ch: E.tensor_tensor(out=xc(VI, ch), in0=xc(VI, ch), in1=YR[:, 0:T], op=OP.add))
                opi("vector", "V", lambda E, ch=ch: E.tensor_scalar(out=xc(ZB, 2 + ch), in0=xc(VI, ch), scalar1=10.0, scalar2=-10.0, op0=OP.min, op1=OP.max))
                ms = opi("vector", "V", lambda E, ch=ch: E.tensor_scalar(out=STI[:, ch:ch + 1], in0=VI[:, ch * T + T - 1: ch * T + T], scalar1=10.0, scalar2=-10.0, op0=OP.min, op1=OP.max))
                st["post%d" % ch] = ms
            steps.append(lambda: s_sp(0))
            steps.append(lambda: s_sp(1))

            def s_wfrom(ocs):
                for oc in ocs:
                    w("tensor", [st.get("b7free"), st["post1"]])
                    if j == 0:
                        w("tensor", ms_wfr)
                    mmm = None
                    for c in range(4):
                        mmm = opi("tensor", "T", lambda E, c=c, oc=oc: E.matmul(
                            PB[7][:], WFR[:, c * 1024 + oc * 128: c * 1024 + (oc + 1) * 128], xc(ZB, c),
                            start=(c == 0), stop=(c == 3)))
                    w("vector", [mmm, st.get("x2_free"), st.get("p2f_%d" % oc)])
                    ms = opi("vector", "V", lambda E, oc=oc, t0=t0: E.scalar_tensor_tensor(
                        out=xc(X2, oc), in0=PB[7][:], scalar=0.1, in1=xc(X0, oc), op0=OP.mult, op1=OP.add))
                    st["b7free"] = ms
                    st["x2_%d" % oc] = ms
            for oc in range(KC):
                steps.append(lambda oc=oc: s_wfrom((oc,)))
            return steps

        def emit_ffn1(i, ms_ln3, ln3_bf):
            BFIN = BFINS[i % 2]
            g_ms = [None] * 32
            for fr in range(8):
                jj, slot, msl = emit_slab_load(scr_w1[fr], ("w1", fr))
                w("tensor", msl)
                last_in_slab = None
                for fc4 in range(4):
                    fc = fr * 4 + fc4
                    b = 2 + fc4
                    w("tensor", bank_free[b])
                    mmm = None
                    for k in range(KC):
                        if fr == 0 and fc4 == 0:
                            w("tensor", ln3_bf[k])
                        mmm = opi("tensor", "T", lambda E, k=k, fc4=fc4, b=b, slot=slot: E.matmul(
                            PB[b][:], slot[:, k * 512 + fc4 * 128: k * 512 + (fc4 + 1) * 128], xc(BFIN, k),
                            start=(k == 0), stop=(k == KC - 1)))
                    bank_ready[b] = mmm
                    last_in_slab = mmm
                    w("scalar", mmm)
                    g_ms[fc] = opi("scalar", "S", lambda E, b=b, fc=fc: E.activation(out=xc(G, fc), in_=PB[b][:], func=AF.Gelu, bias=BHV[:, fc:fc + 1]))
                    bank_free[b] = g_ms[fc]
                ring_set_free(jj, last_in_slab)
                if fr == 3 and pending_p2[0] is not None:
                    pending_p2[0]()
                    pending_p2[0] = None
            return g_ms

        def emit_ffn2(i, g_ms, u_ms):
            x3_ms = [None] * KC
            if i == 0:
                w("tensor", ms_b2c)
            for fr in range(2):
                for kr in range(4):
                    jj, slot, msl = emit_slab_load(scr_w2[fr * 4 + kr], ("w2", fr * 4 + kr))
                    w("tensor", msl)
                    for jx in range(8):
                        w("tensor", g_ms[kr * 8 + jx])
                    mm_last = None
                    for oc4 in range(4):
                        oc = fr * 4 + oc4
                        b = 2 + oc4
                        if kr == 0:
                            w("tensor", bank_free[b])
                            opi("tensor", "T", lambda E, b=b, oc=oc: E.matmul(
                                PB[b][:], B2R[0:1, oc * 128:(oc + 1) * 128], ONER[:], start=True, stop=False))
                        for jx in range(8):
                            mm_last = opi("tensor", "T", lambda E, jx=jx, kr=kr, oc4=oc4, b=b, slot=slot: E.matmul(
                                PB[b][:], slot[:, jx * 512 + oc4 * 128: jx * 512 + (oc4 + 1) * 128], xc(G, kr * 8 + jx),
                                start=False, stop=(kr == 3 and jx == 7)))
                        if kr == 3:
                            bank_ready[b] = mm_last
                    ring_set_free(jj, mm_last)
                for oc4 in range(4):
                    oc = fr * 4 + oc4
                    b = 2 + oc4
                    w("vector", bank_ready[b])
                    x3_ms[oc] = opi("vector", "V", lambda E, oc=oc, b=b: E.scalar_tensor_tensor(
                        out=xc(X2, oc), in0=PB[b][:], scalar=0.1, in1=xc(X2, oc), op0=OP.mult, op1=OP.add))
                    bank_free[b] = x3_ms[oc]
                    u_ms[oc] = opi("vector", "V", lambda E, oc=oc: E.tensor_tensor(
                        out=xc(X2, oc), in0=xc(X0, oc), in1=xc(X2, oc), op=OP.add))
            return x3_ms

        last_gu_mm = [None]
        prev_out_so = [None]
        pending_p2 = [None]

        for s in make_chain(0):
            s()

        for i in range(NT):
            t0 = i * T
            x2_last = st["x2_7"]
            ms_ln3, _, _, ln3_bf, _ = emit_ln(lambda k: xc(X2, k), i % 2, gv=G3V, bv=B3V, src_ready=x2_last)
            g_ms = emit_ffn1(i, ms_ln3, ln3_bf)
            u_ms = [None] * KC
            x3_ms = emit_ffn2(i, g_ms, u_ms)
            if i + 1 < NT:
                emit_xload(i + 1, extra=u_ms[KC - 1])
            ms_p1bf, ms_p1f, ms_p1sq, p1_bf, p1_f = emit_ln(lambda k: xc(X2, k), i % 2, dstf=lambda k: xc(ACC, k), out_bf=True,
                                                            src_ready=u_ms, extra_vwait=prev_out_so[0], twopass=True)
            st["x2_free"] = ms_p1sq
            BFIN = BFINS[i % 2]
            w("tensor", bank_free[6])
            if i == 0:
                w("tensor", [ms_rt, ms_srb, ms_rto])
            lg_last = None
            for tg in range(4):
                for k in range(KC):
                    if tg == 0:
                        w("tensor", ms_p1bf[k])
                    lg_last = opi("tensor", "T", lambda E, tg=tg, k=k, BFIN=BFIN: E.matmul(
                        PB[6][:, tg * 9:(tg + 1) * 9], xc(BFIN, k)[:, tg * 128:(tg + 1) * 128], RT[:, k * 9:(k + 1) * 9],
                        start=(k == 0), stop=(k == KC - 1)))
            SQB2 = BFINS[(i + 1) % 2]
            for tg in range(4):
                for k in range(KC):
                    if tg == 0:
                        w("tensor", st["p1_sq"][k])
                    lg_last = opi("tensor", "T", lambda E, tg=tg, k=k, SQB2=SQB2: E.matmul(
                        PB[6][:, 36 + tg:37 + tg], xc(SQB2, k)[:, tg * 128:(tg + 1) * 128],
                        RT[:, 8:9],
                        start=(k == 0), stop=(k == KC - 1)))
            P63 = PB[6][:, 0:36].rearrange("p (g c) -> p g c", g=4)
            w("vector", lg_last)
            opi("vector", "V", lambda E: E.tensor_scalar(out=CBE[:, 0:4], in0=P63[:, :, 8], scalar1=float(1.0 / D), scalar2=None, op0=OP.mult))
            opi("vector", "V", lambda E: E.tensor_tensor(out=CBE[:, 8:12], in0=CBE[:, 0:4], in1=CBE[:, 0:4], op=OP.mult))
            opi("vector", "V", lambda E: E.scalar_tensor_tensor(out=CBE[:, 4:8], in0=PB[6][:, 36:40], scalar=float(1.0 / D), in1=CBE[:, 8:12], op0=OP.mult, op1=OP.subtract))
            ms_veps2 = opi("vector", "V", lambda E: E.tensor_scalar(out=CBE[:, 4:8], in0=CBE[:, 4:8], scalar1=float(EPS), scalar2=None, op0=OP.add))
            w("scalar", ms_veps2)
            ms_sq2 = opi("scalar", "S", lambda E: E.activation(out=CBE[:, 4:8], in_=CBE[:, 4:8], func=AF.Sqrt))
            w("vector", ms_sq2)
            opi("vector", "V", lambda E: E.reciprocal(out=CBE[:, 4:8], in_=CBE[:, 4:8]))
            opi("vector", "V", lambda E: E.tensor_tensor(
                out=CBE[:, 8:40].rearrange("p (g c) -> p g c", g=4),
                in0=CBE[:, 0:4].unsqueeze(-1).broadcast_to((128, 4, 8)),
                in1=SRB[:].unsqueeze(1).broadcast_to((128, 4, 8)), op=OP.mult))
            opi("vector", "V", lambda E: E.tensor_tensor(out=LGS[:].rearrange("p (g c) -> p g c", g=4), in0=P63[:, :, 0:8], in1=CBE[:, 8:40].rearrange("p (g c) -> p g c", g=4), op=OP.subtract))
            ms_lgs = opi("vector", "V", lambda E: E.tensor_tensor(
                out=LGS[:].rearrange("p (g c) -> p g c", g=4),
                in0=LGS[:].rearrange("p (g c) -> p g c", g=4),
                in1=CBE[:, 4:8].unsqueeze(-1).broadcast_to((128, 4, 8)), op=OP.mult))
            bank_free[6] = ms_lgs
            for tg in range(4):
                op("vector", lambda E, tg=tg: E.max(out=MX4[:, tg * 8:(tg + 1) * 8], in_=LGS[:, tg * 8:(tg + 1) * 8]))
            MX3 = MX4.rearrange("p (g c) -> p g c", g=4)
            LG3 = LGS[:].rearrange("p (g c) -> p g c", g=4)
            CM3 = CMB[:].rearrange("p (g c) -> p g c", g=4)
            SC3 = ZS[:, 64:96].rearrange("p (g c) -> p g c", g=4)
            ms_d = opi("vector", "V", lambda E: E.tensor_tensor(out=WW4A, in0=MX3[:, :, 1], in1=MX3[:, :, 0], op=OP.subtract))
            w("scalar", ms_d)
            ms_w2 = opi("scalar", "S", lambda E: E.activation(out=WW4B, in_=WW4A, func=AF.Sigmoid))
            w("vector", ms_w2)
            opi("vector", "V", lambda E: E.tensor_scalar(out=WW4A, in0=WW4B, scalar1=-1.0, scalar2=1.0, op0=OP.mult, op1=OP.add))
            opi("vector", "V", lambda E: E.tensor_tensor(out=CM3, in0=LG3, in1=MX3[:, :, 0:1].broadcast_to((128, 4, 8)), op=OP.is_equal))
            opi("vector", "V", lambda E: E.tensor_tensor(out=CM3, in0=CM3, in1=WW4A.unsqueeze(-1).broadcast_to((128, 4, 8)), op=OP.mult))
            opi("vector", "V", lambda E: E.tensor_tensor(out=SC3, in0=LG3, in1=MX3[:, :, 1:2].broadcast_to((128, 4, 8)), op=OP.is_equal))
            opi("vector", "V", lambda E: E.tensor_tensor(out=SC3, in0=SC3, in1=WW4B.unsqueeze(-1).broadcast_to((128, 4, 8)), op=OP.mult))
            cmb_ms = opi("vector", "V", lambda E: E.tensor_tensor(out=CMB[:], in0=CMB[:], in1=ZS[:, 64:96], op=OP.add))
            w("tensor", [cmb_ms, ms_lgs, ms_sl if i == 0 else None])
            tr_last = None
            for tg in range(4):
                tr_last = opi("tensor", "T", lambda E, tg=tg: E.transpose(
                    PB[6][0:8, tg * 128:(tg + 1) * 128], CMB[:, tg * 8:(tg + 1) * 8], EYE[:]))
            w("scalar", tr_last)
            ms_cte = opi("scalar", "S", lambda E: E.activation(out=CTE[:], in_=PB[6][0:8, 0:T], func=AF.Copy))
            bank_free[6] = ms_cte
            st["p1_pass1"]()

            st["accinit"] = None

            bg = list(st.pop("p1_pass2_steps", []))
            if i + 1 < NT:
                bg += list(make_chain(i + 1))

            def bg_pop(n=1):
                for _ in range(n):
                    if bg:
                        bg.pop(0)()

            prev_upe = [None, None]
            prev_prd = [None] * 8
            prev_cbe_readers = ms_lgs
            acc_ms = p1_f
            for e in range(NE):
                w("tensor", [ms_cte if e == 0 else None, ms_eh if (i == 0 and e == 0) else None])
                w("tensor", bank_free[6])
                ms_cb = opi("tensor", "T", lambda E, e=e: E.matmul(
                    PB[6][:], EH[0:8, e * 128:(e + 1) * 128], CTE[:], start=True, stop=True))
                w("scalar", [ms_cb, prev_cbe_readers])
                ms_cbe = opi("scalar", "S", lambda E: E.activation(out=CBE[:], in_=PB[6][:], func=AF.Copy))
                bank_free[6] = ms_cbe
                gu_last = None
                sg_ms = [None] * 8
                last_up_tt = None
                for fr in (0, 2, 1, 3):
                    jj, slot, msl = emit_slab_load(scr_gu[e, fr], ("gu", e, fr))
                    w("tensor", msl)
                    gu_last_slab = None
                    for fc4 in range(4):
                        gc = fr * 4 + fc4
                        b = 2 + fc4
                        w("tensor", bank_free[b])
                        mmm = None
                        for k in range(KC):
                            if e == 0 and fr == 0 and fc4 == 0:
                                w("tensor", p1_bf[k])
                            mmm = opi("tensor", "T", lambda E, k=k, fc4=fc4, b=b, slot=slot, BFIN=BFIN: E.matmul(
                                PB[b][:], slot[:, k * 512 + fc4 * 128: k * 512 + (fc4 + 1) * 128], xc(BFIN, k),
                                start=(k == 0), stop=(k == KC - 1)))
                        bank_ready[b] = mmm
                        gu_last = gu_last_slab = mmm
                        if gc < 8:
                            w("scalar", [mmm, prev_prd[gc]])
                            sg_ms[gc] = opi("scalar", "S", lambda E, b=b, gc=gc: E.activation(out=xc(SG, gc), in_=PB[b][:], func=AF.Silu))
                            bank_free[b] = sg_ms[gc]
                        else:
                            uc = gc - 8
                            par = uc % 2
                            w("vector", [mmm, ms_cbe, prev_upe[par]])
                            ms_up = opi("vector", "V", lambda E, b=b, par=par: E.tensor_tensor(out=xc(UPE, par), in0=PB[b][:], in1=CBE[:], op=OP.mult))
                            bank_free[b] = ms_up
                            last_up_tt = ms_up
                            peng = "gpsimd" if (uc % 2 == 0) else "vector"
                            psem = "G" if (uc % 2 == 0) else "V"
                            w(peng, [ms_up, sg_ms[uc]])
                            prev_upe[par] = opi(peng, psem, lambda E, uc=uc, par=par: E.tensor_tensor(out=xc(PRD, uc), in0=xc(SG, uc), in1=xc(UPE, par), op=OP.mult))
                            prev_prd[uc] = prev_upe[par]
                    ring_set_free(jj, gu_last_slab)
                    bg_pop()
                last_gu_mm[0] = gu_last
                bfin_user[i % 2] = gu_last
                prev_cbe_readers = last_up_tt
                if e == 0:
                    bg_pop(4)
                for fr in range(2):
                    jj, slot, msl = emit_slab_load(scr_dp[e, fr], ("dp", e, fr))
                    w("tensor", msl)
                    if fr == 0:
                        w("tensor", [prev_prd[k] for k in range(8)])
                    dn_last_slab = None
                    for oc4 in range(4):
                        oc = fr * 4 + oc4
                        b = 2 + oc4
                        w("tensor", bank_free[b])
                        mmm = None
                        for k in range(KC):
                            mmm = opi("tensor", "T", lambda E, k=k, oc4=oc4, b=b, slot=slot: E.matmul(
                                PB[b][:], slot[:, k * 512 + oc4 * 128: k * 512 + (oc4 + 1) * 128], xc(PRD, k),
                                start=(k == 0), stop=(k == KC - 1)))
                        bank_ready[b] = mmm
                        dn_last_slab = mmm
                        w("vector", [mmm, acc_ms[oc]])
                        acc_ms[oc] = opi("vector", "V", lambda E, oc=oc, b=b: E.tensor_tensor(out=xc(ACC, oc), in0=PB[b][:], in1=xc(ACC, oc), op=OP.add))
                        bank_free[b] = acc_ms[oc]
                    ring_set_free(jj, dn_last_slab)
                    bg_pop()

            while bg:
                bg.pop(0)()

            def make_p2(i=i, t0=t0, acc_ms=list(acc_ms), gu_ms=last_gu_mm[0]):
                def p2():
                    _, ms_out, _, _, _ = emit_ln(lambda k: xc(ACC, k), i % 2, dstf=lambda k: xc(ACC, k), out_bf=False,
                                                 src_ready=acc_ms[KC - 1], extra_swait=gu_ms, apply_gp=True)
                    w("sync", ms_out)
                    for k in range(KC):
                        opd("sync", "SO", lambda E, k=k, t0=t0: E.dma_start(out=outTr[k, :, t0:t0 + T], in_=xc(ACC, k)))
                    prev_out_so[0] = ("SO", cnt["SO"])
                return p2
            pending_p2[0] = make_p2()
            if i == NT - 1:
                pending_p2[0]()
                pending_p2[0] = None

        for sk in ("SO", "SX", "SR", "R0", "R1", "R2", "R3"):
            if cnt[sk]:
                w("sync", (sk, cnt[sk]))

        block = es.enter_context(nc.Block())
        for eng_name, dec in (("sync", block.sync), ("tensor", block.tensor),
                              ("vector", block.vector), ("scalar", block.scalar),
                              ("gpsimd", block.gpsimd)):
            def body(E, eng_name=eng_name):
                for fn in q[eng_name]:
                    fn(E)
            dec(body)
    return nc


_CACHE = {}


def _get_nc(stop_stage):
    if stop_stage not in _CACHE:
        _CACHE[stop_stage] = build(stop_stage)
    return _CACHE[stop_stage]


def kernel(L0_attn_x, L0_attn_W_to, L0_attn_log_decay, L0_attn_frequencies,
           L0_attn_W_from, L0_attn_ln1_g, L0_attn_ln1_b, L0_attn_ln3_g,
           L0_attn_ln3_b, L0_attn_W_ffn1, L0_attn_b_ffn1, L0_attn_W_ffn2,
           L0_attn_b_ffn2, L0_ffn_router_weight, L0_ffn_gate_up_proj,
           L0_ffn_down_proj):
    x = np.asarray(L0_attn_x, dtype=np.float32)
    B = x.shape[0]
    C = np.ascontiguousarray
    shared = {
        "wtoT": C(np.asarray(L0_attn_W_to, np.float32).T),
        "wfromT": C(np.asarray(L0_attn_W_from, np.float32).T),
        "w1T": C(np.asarray(L0_attn_W_ffn1, np.float32).T),
        "w2T": C(np.asarray(L0_attn_W_ffn2, np.float32).T),
        "rtT": C(np.asarray(L0_ffn_router_weight, np.float32).T),
        "guT": C(np.asarray(L0_ffn_gate_up_proj, np.float32).transpose(0, 2, 1)),
        "dpT": C(np.asarray(L0_ffn_down_proj, np.float32).transpose(0, 2, 1)),
        "ln1g": C(np.asarray(L0_attn_ln1_g, np.float32)),
        "ln1b": C(np.asarray(L0_attn_ln1_b, np.float32)),
        "ln3g": C(np.asarray(L0_attn_ln3_g, np.float32)),
        "ln3b": C(np.asarray(L0_attn_ln3_b, np.float32)),
        "bf1": C(np.asarray(L0_attn_b_ffn1, np.float32)),
        "bf2": C(np.asarray(L0_attn_b_ffn2, np.float32)),
        "ldec": C(np.asarray(L0_attn_log_decay, np.float32)),
        "freq": C(np.asarray(L0_attn_frequencies, np.float32)),
        "eye_in": np.eye(128, dtype=np.float32),
    }
    in_maps = []
    for c in range(B):
        m = dict(shared)
        m["xT"] = C(x[c].T)
        in_maps.append(m)
    nc = _get_nc(STOP_STAGE)
    res = run_bass_kernel_spmd(nc, in_maps, core_ids=list(range(B)))
    out = np.stack([np.asarray(r["outT"]).T for r in res.results], axis=0)
    return out.astype(np.float32)


# version 20 (final)
# speedup vs baseline: 1.1855x; 1.1855x over previous
import numpy as np
from contextlib import ExitStack

import concourse.bass as bass
import concourse.mybir as mybir
from concourse.bass_utils import run_bass_kernel_spmd
from concourse.alu_op_type import AluOpType as OP

F32 = mybir.dt.float32
BF16 = mybir.dt.bfloat16
AF = mybir.ActivationFunctionType

D = 1024
S = 4096
T = 512
NT = S // T
KC = D // 128
FS = 256
DFF = 4096
NE = 8
GUF = 2048
EPS = 1e-5
RSQ2 = float(1.0 / np.sqrt(2.0))

STOP_STAGE = "full"
DBG = False

ENGS = ("tensor", "vector", "scalar", "gpsimd", "sync")


def build(stop_stage="full"):
    nc = bass.Bass("TRN2", target_bir_lowering=False)

    xT = nc.dram_tensor("xT", [D, S], F32, kind="ExternalInput").ap()
    wtoT = nc.dram_tensor("wtoT", [D, 2 * FS], F32, kind="ExternalInput").ap()
    wfromT = nc.dram_tensor("wfromT", [2 * FS, D], F32, kind="ExternalInput").ap()
    w1T = nc.dram_tensor("w1T", [D, DFF], F32, kind="ExternalInput").ap()
    w2T = nc.dram_tensor("w2T", [DFF, D], F32, kind="ExternalInput").ap()
    rtT = nc.dram_tensor("rtT", [D, NE], F32, kind="ExternalInput").ap()
    guT = nc.dram_tensor("guT", [NE, D, GUF], F32, kind="ExternalInput").ap()
    dpT = nc.dram_tensor("dpT", [NE, D, D], F32, kind="ExternalInput").ap()
    ln1g = nc.dram_tensor("ln1g", [D], F32, kind="ExternalInput").ap()
    ln1b = nc.dram_tensor("ln1b", [D], F32, kind="ExternalInput").ap()
    ln3g = nc.dram_tensor("ln3g", [D], F32, kind="ExternalInput").ap()
    ln3b = nc.dram_tensor("ln3b", [D], F32, kind="ExternalInput").ap()
    bf1 = nc.dram_tensor("bf1", [DFF], F32, kind="ExternalInput").ap()
    bf2 = nc.dram_tensor("bf2", [D], F32, kind="ExternalInput").ap()
    ldec = nc.dram_tensor("ldec", [FS], F32, kind="ExternalInput").ap()
    freq = nc.dram_tensor("freq", [FS], F32, kind="ExternalInput").ap()
    eye_in = nc.dram_tensor("eye_in", [128, 128], F32, kind="ExternalInput").ap()
    outT = nc.dram_tensor("outT", [D, S], F32, kind="ExternalOutput").ap()

    scr_w1 = nc.dram_tensor("scr_w1", [8, 128, 4096], BF16, kind="Internal").ap()
    scr_w2 = nc.dram_tensor("scr_w2", [8, 128, 4096], BF16, kind="Internal").ap()
    scr_gu = nc.dram_tensor("scr_gu", [NE, 4, 128, 4096], BF16, kind="Internal").ap()
    scr_dp = nc.dram_tensor("scr_dp", [NE, 2, 128, 4096], BF16, kind="Internal").ap()

    es = ExitStack()

    def sb(name, shape, dt=F32):
        return es.enter_context(nc.sbuf_tensor(name, shape, dt))

    def psum(name, shape, dt=F32):
        return es.enter_context(nc.psum_tensor(name, shape, dt))

    with es:
        X0 = sb("X0", [128, KC * T])
        X2 = sb("X2", [128, KC * T])
        ACC = sb("ACC", [128, KC * T])
        BFINS = [sb("BFIN0", [128, KC * T], BF16), sb("BFIN1", [128, KC * T], BF16)]
        VR = sb("VR", [128, 2 * T]);  VI = sb("VI", [128, 2 * T])
        YR = sb("YR", [128, T]);  YI = sb("YI", [128, T])
        ZS = sb("ZS", [128, T])
        ZB = sb("ZB", [128, 4 * T], BF16)
        DFU = sb("DFU", [128, 2 * T])
        TBW = 514
        CF = sb("CF", [128, 2 * TBW])
        SF = sb("SF", [128, 2 * TBW])
        STR = sb("STR", [128, 2]); STI = sb("STI", [128, 2])
        CM1 = sb("CM1", [128, 1]); SM1 = sb("SM1", [128, 1])
        G = sb("G", [128, 32 * T], BF16)
        SG = sb("SG", [128, 8 * T], BF16)
        MEANT = sb("MEANT", [128, T])
        RSTD = sb("RSTD", [128, T])
        NEGH = sb("NEGH", [128, 1])
        UPE = sb("UPE", [128, 2 * T])
        PRD = sb("PRD", [128, 8 * T], BF16)
        CBE = sb("CBE", [128, T])
        WTO = sb("WTO", [128, KC * 512], BF16)
        WFR = sb("WFR", [128, 4 * 1024], BF16)
        RT = sb("RT", [128, KC * 9], BF16)
        RING = sb("RING", [128, 4 * 4096], BF16)
        ONESB = sb("ONESB", [128, 128], BF16)
        ONER = sb("ONER", [1, T], BF16)
        EYE = sb("EYE", [128, 128])
        EH = sb("EH", [8, NE * 128], BF16)
        B2R = sb("B2R", [1, D], BF16)
        G1V = sb("G1V", [128, KC]); B1V = sb("B1V", [128, KC])
        G3V = sb("G3V", [128, KC]); B3V = sb("B3V", [128, KC])
        BHV = sb("BHV", [128, 32])
        LDV = sb("LDV", [128, 2]); FRV = sb("FRV", [128, 2])
        DEC = sb("DEC", [128, 2]); PI2 = sb("PI2", [128, 1])
        LGS = sb("LGS", [128, 32])
        SRB = sb("SRB", [128, 8])
        CMB = sb("CMB", [128, 32])
        CTE = sb("CTE", [8, T], BF16)
        MX4 = ZS[:, 0:32]; WW4A = ZS[:, 32:36]; WW4B = ZS[:, 36:40]

        PB = [psum("PB%d" % i, [128, 512]) for i in range(8)]

        sems = {}
        for name in ("SX", "R0", "R1", "R2", "R3", "SR", "SO", "T", "V", "S", "G"):
            sems[name] = es.enter_context(nc.semaphore("sem_" + name))
        STEP = {"SX": 16, "R0": 16, "R1": 16, "R2": 16, "R3": 16, "SR": 16, "SO": 16,
                "T": 1, "V": 1, "S": 1, "G": 1}

        q = {k: [] for k in ENGS}
        cnt = {k: 0 for k in sems}

        def w(eng, ms):
            if ms is None:
                return
            if isinstance(ms, list):
                for m in ms:
                    w(eng, m)
                return
            sk, v = ms
            q[eng].append(lambda E, s=sems[sk], vv=v: E.wait_ge(s, vv))

        DRAIN_ENGS = ("vector",)

        def op(eng, fn):
            q[eng].append(fn)
            if eng in DRAIN_ENGS:
                q[eng].append(lambda E: E.drain())

        def opi(eng, sk, fn):
            cnt[sk] += STEP[sk]
            v = cnt[sk]
            q[eng].append(lambda E, fn=fn, s=sems[sk], st=STEP[sk]: fn(E).then_inc(s, st))
            if eng in DRAIN_ENGS:
                q[eng].append(lambda E: E.drain())
            return (sk, v)

        def opd(eng, sk, fn):
            cnt[sk] += STEP[sk]
            v = cnt[sk]
            q[eng].append(lambda E, fn=fn, s=sems[sk], st=STEP[sk]: fn(E).then_inc(s, st))
            return (sk, v)

        def vmark():
            return ("V", cnt["V"])

        def xc(buf, k, width=T):
            return buf[:, k * width:(k + 1) * width]

        bank_free = [None] * 8
        bank_ready = [None] * 8

        NRS = 4
        ring_free = [None] * NRS
        ring_idx = [0]

        def ring_slot():
            j = ring_idx[0] % NRS
            ring_idx[0] += 1
            ms = ring_free[j]
            ring_free[j] = None
            return j, RING[:, j * 4096:(j + 1) * 4096], ms

        def ring_set_free(j, ms):
            if isinstance(ring_free[j], list):
                ring_free[j] = ring_free[j] + [ms]
            else:
                ring_free[j] = ms

        def slot_dma(j, fn):
            return opd("sync", "R%d" % j, fn)

        op("vector", lambda E: E.memset(ONESB[:], 1.0))
        op("vector", lambda E: E.memset(ONER[:], 1.0))
        op("vector", lambda E: E.memset(PI2[:], np.pi / 2))
        op("vector", lambda E: E.memset(STR[:], 0.0))
        op("vector", lambda E: E.memset(STI[:], 0.0))
        op("vector", lambda E: E.memset(NEGH[:], -0.5))
        op("gpsimd", lambda E: E.iota(out=EH[:], pattern=[[-1, NE], [0, 128]], base=0, channel_multiplier=1, allow_small_or_imprecise_dtypes=True))
        op("gpsimd", lambda E: E.drain())
        ms_eh = opi("gpsimd", "G", lambda E: E.tensor_scalar(out=EH[:], in0=EH[:], scalar1=0.0, scalar2=None, op0=OP.is_equal))

        jb, slotb, msfb = ring_slot()
        w("sync", msfb)
        ms_b2 = slot_dma(jb, lambda E, d=slotb.bitcast(F32)[0:1, 0:D],
                         s=bf2.rearrange("(a d) -> a d", a=1): E.dma_start(out=d, in_=s))
        w("vector", ms_b2)
        ms_b2c = opi("vector", "V", lambda E, s=slotb.bitcast(F32)[0:1, 0:D]: E.tensor_copy(out=B2R[:], in_=s))
        ring_set_free(jb, ms_b2c)

        small_loads = [
            (EYE[:], eye_in[:]),
            (G1V[:], ln1g.rearrange("(k p) -> p k", p=128)),
            (B1V[:], ln1b.rearrange("(k p) -> p k", p=128)),
            (G3V[:], ln3g.rearrange("(k p) -> p k", p=128)),
            (B3V[:], ln3b.rearrange("(k p) -> p k", p=128)),
            (BHV[:], bf1.rearrange("(k p) -> p k", p=128)),
            (LDV[:], ldec.rearrange("(k p) -> p k", p=128)),
            (FRV[:], freq.rearrange("(k p) -> p k", p=128)),
        ]
        ms_sl = None
        for dst, src in small_loads:
            ms_sl = opd("scalar", "SR", lambda E, d=dst, s=src: E.dma_start(out=d, in_=s, allow_slow_non_contiguous=True))

        ms_bh = ms_b2c

        w("scalar", ms_sl)
        ms_dec = opi("scalar", "S", lambda E: E.activation(out=DEC[:], in_=LDV[:], func=AF.Sigmoid))
        ms_th = opi("scalar", "S", lambda E: E.activation(out=FRV[:], in_=FRV[:], func=AF.Tanh))
        op("scalar", lambda E: E.drain())

        def cfc(ch, a, b):
            return CF[:, ch * TBW + a: ch * TBW + b]

        def sfc(ch, a, b):
            return SF[:, ch * TBW + a: ch * TBW + b]

        w("scalar", [ms_th, ms_bh])
        ms_seed = None
        for ch in range(2):
            opi("scalar", "S", lambda E, ch=ch: E.activation(out=cfc(ch, 1, 2), in_=FRV[:, ch:ch + 1], func=AF.Sin, scale=0.1, bias=PI2[:]))
            ms_seed = opi("scalar", "S", lambda E, ch=ch: E.activation(out=sfc(ch, 1, 2), in_=FRV[:, ch:ch + 1], func=AF.Sin, scale=0.1))
        for ch in range(2):
            op("vector", lambda E, ch=ch: E.memset(cfc(ch, 0, 1), 1.0))
            op("vector", lambda E, ch=ch: E.memset(sfc(ch, 0, 1), 0.0))
        w("vector", ms_seed)
        ms_tab = None
        for ch in range(2):
            K = 2
            while K < T + 1:
                n = min(K - 1, T + 1 - K)
                opi("vector", "V", lambda E, ch=ch, K=K: E.tensor_copy(out=CM1[:], in_=cfc(ch, K - 1, K)))
                opi("vector", "V", lambda E, ch=ch, K=K: E.tensor_copy(out=SM1[:], in_=sfc(ch, K - 1, K)))
                op("vector", lambda E: E.drain())
                opi("vector", "V", lambda E, ch=ch, K=K, n=n: E.tensor_scalar(out=cfc(ch, K, K + n), in0=cfc(ch, 1, 1 + n), scalar1=CM1[:], scalar2=None, op0=OP.mult))
                opi("vector", "V", lambda E, ch=ch, n=n: E.tensor_scalar(out=YR[:, 0:n], in0=sfc(ch, 1, 1 + n), scalar1=SM1[:], scalar2=None, op0=OP.mult))
                opi("vector", "V", lambda E, ch=ch, K=K, n=n: E.tensor_tensor(out=cfc(ch, K, K + n), in0=cfc(ch, K, K + n), in1=YR[:, 0:n], op=OP.subtract))
                opi("vector", "V", lambda E, ch=ch, K=K, n=n: E.tensor_scalar(out=sfc(ch, K, K + n), in0=sfc(ch, 1, 1 + n), scalar1=CM1[:], scalar2=None, op0=OP.mult))
                opi("vector", "V", lambda E, ch=ch, n=n: E.tensor_scalar(out=YR[:, 0:n], in0=cfc(ch, 1, 1 + n), scalar1=SM1[:], scalar2=None, op0=OP.mult))
                ms_tab = opi("vector", "V", lambda E, ch=ch, K=K, n=n: E.tensor_tensor(out=sfc(ch, K, K + n), in0=sfc(ch, K, K + n), in1=YR[:, 0:n], op=OP.add))
                K += n
        w("vector", ms_dec)
        op("vector", lambda E: E.memset(DFU[:], 1.0))
        ms_dfu = None
        for ch in range(2):
            ms_dfu = opi("vector", "V", lambda E, ch=ch: E.tensor_scalar(out=xc(DFU, ch), in0=xc(DFU, ch), scalar1=DEC[:, ch:ch + 1], scalar2=None, op0=OP.mult))

        def prepass_unit(srcs, dst_sb=None, dst_dram=None, scale=None, ncols=2048):
            halves = []
            for hv in range(len(srcs)):
                jj, slot, msf = ring_slot()
                w("sync", msf)
                g = srcs[hv].shape[0]
                gc_ = ncols // g
                dview = slot.bitcast(F32)[:, 0:ncols].rearrange("p (k f) -> p k f", k=g)
                sview = srcs[hv].rearrange("k p f -> p k f")
                ms_rd = slot_dma(jj, lambda E, d=dview, s=sview: E.dma_start(out=d, in_=s))
                halves.append((jj, slot, ms_rd))
            CENG = (("vector", "V"), ("scalar", "S"))
            if dst_sb is not None:
                ms_cs = []
                for hv, (jj, slot, ms_rd) in enumerate(halves):
                    eng, sk = CENG[hv % 2]
                    w(eng, ms_rd)
                    dstap = dst_sb[:, hv * ncols:(hv + 1) * ncols]
                    if eng == "scalar":
                        ms_c = opi(eng, sk, lambda E, d=dstap, s=slot.bitcast(F32)[:, 0:ncols], sc=(scale or 1.0): E.activation(out=d, in_=s, func=AF.Copy, scale=sc))
                    elif scale is None:
                        ms_c = opi(eng, sk, lambda E, d=dstap, s=slot.bitcast(F32)[:, 0:ncols]: E.tensor_copy(out=d, in_=s))
                    else:
                        ms_c = opi(eng, sk, lambda E, d=dstap, s=slot.bitcast(F32)[:, 0:ncols], sc=scale: E.tensor_scalar(out=d, in0=s, scalar1=sc, scalar2=None, op0=OP.mult))
                    ring_set_free(jj, ms_c)
                    ms_cs.append(ms_c)
                return ms_cs, None
            jo, oslot, msfo = ring_slot()
            ms_cs = []
            for hv, (jj, slot, ms_rd) in enumerate(halves):
                eng, sk = CENG[hv % 2]
                w(eng, ms_rd)
                if hv == 0:
                    w(eng, msfo)
                else:
                    w(eng, msfo)
                dstap = oslot[:, hv * ncols:(hv + 1) * ncols]
                if eng == "scalar":
                    ms_c = opi(eng, sk, lambda E, d=dstap, s=slot.bitcast(F32)[:, 0:ncols]: E.activation(out=d, in_=s, func=AF.Copy))
                else:
                    ms_c = opi(eng, sk, lambda E, d=dstap, s=slot.bitcast(F32)[:, 0:ncols]: E.tensor_copy(out=d, in_=s))
                ring_set_free(jj, ms_c)
                ms_cs.append(ms_c)
            w("sync", ms_cs)
            ms_wr = slot_dma(jo, lambda E, d=dst_dram, s=oslot[:]: E.dma_start(out=d, in_=s))
            return ms_cs, ms_wr, jo, oslot

        wtoR = wtoT.rearrange("(k p) f -> k p f", p=128)
        wfromR = wfromT.rearrange("(c p) d -> c p d", p=128)
        w1R = w1T.rearrange("(k p) f -> k p f", p=128)
        w2R = w2T.rearrange("(k p) d -> k p d", p=128)
        guR = guT.rearrange("e (k p) f -> e k p f", p=128)
        dpR = dpT.rearrange("e (k p) d -> e k p d", p=128)
        rtR = rtT.rearrange("(k p) e -> k p e", p=128)

        xTr = xT.rearrange("(k p) t -> k p t", p=128)
        outTr = outT.rearrange("(k p) t -> k p t", p=128)
        x_ms = [None] * (NT + 1)

        def emit_xload(i, extra=None):
            t0 = i * T
            w("sync", extra)
            ms = None
            for k in range(KC):
                ms = opd("sync", "SX", lambda E, k=k, t0=t0: E.dma_start(out=xc(X0, k), in_=xTr[k, :, t0:t0 + T]))
            x_ms[i] = ms

        emit_xload(0)

        ms_wto, _ = prepass_unit(
            [wtoR[0:4], wtoR[4:8]],
            dst_sb=WTO[:], scale=0.1)
        ms_wfr, _ = prepass_unit(
            [wfromR[0:2], wfromR[2:4]],
            dst_sb=WFR[:])
        jr, slot_r, msf = ring_slot()
        w("sync", msf)
        ms_rt_rd = None
        for kk in range(KC):
            ms_rt_rd = slot_dma(jr, lambda E, d=slot_r.bitcast(F32)[:, kk * 8:(kk + 1) * 8], s=rtR[kk]: E.dma_start(out=d, in_=s))
        w("gpsimd", ms_rt_rd)
        ms_rt = None
        for kk in range(KC):
            ms_rt = opi("gpsimd", "G", lambda E, kk=kk, s=slot_r.bitcast(F32)[:, kk * 8:(kk + 1) * 8]: E.tensor_copy(out=RT[:, kk * 9:kk * 9 + 8], in_=s))
        ring_set_free(jr, ms_rt)
        ms_rto = opi("vector", "V", lambda E: E.memset(RT[:].rearrange("p (k e) -> p k e", k=KC)[:, :, 8], 1.0))
        w("tensor", [ms_rt, ms_b2c, ms_rto])
        ms_srb_mm = None
        for kk in range(KC):
            ms_srb_mm = opi("tensor", "T", lambda E, kk=kk: E.matmul(PB[6][:, 0:8], ONESB[:], RT[:, kk * 9:kk * 9 + 8], start=(kk == 0), stop=(kk == KC - 1)))
        w("scalar", ms_srb_mm)
        ms_srb = opi("scalar", "S", lambda E: E.activation(out=SRB[:], in_=PB[6][:, 0:8], func=AF.Copy))
        bank_free[6] = ms_srb

        scr_ready = {}

        def _scr_srcs(key):
            if key[0] == "w1":
                fr = key[1]
                return ([w1R[0:4, :, fr * 512:(fr + 1) * 512],
                         w1R[4:8, :, fr * 512:(fr + 1) * 512]],
                        scr_w1[fr])
            if key[0] == "w2":
                fr, kr = key[1] // 4, key[1] % 4
                return ([w2R[kr * 8:kr * 8 + 4, :, fr * 512:(fr + 1) * 512],
                         w2R[kr * 8 + 4:kr * 8 + 8, :, fr * 512:(fr + 1) * 512]],
                        scr_w2[key[1]])
            if key[0] == "gu":
                _, e, fr = key
                return ([guR[e, 0:4, :, fr * 512:(fr + 1) * 512],
                         guR[e, 4:8, :, fr * 512:(fr + 1) * 512]],
                        scr_gu[e, fr])
            _, e, fr = key
            return ([dpR[e, 0:4, :, fr * 512:(fr + 1) * 512],
                     dpR[e, 4:8, :, fr * 512:(fr + 1) * 512]],
                    scr_dp[e, fr])




        def emit_slab_load(dram_ap, key):
            if key not in scr_ready:
                srcs, dram = _scr_srcs(key)
                ms_cs, mw, jo, oslot = prepass_unit(srcs, dst_dram=dram)
                scr_ready[key] = mw
                ring_free[jo] = [mw]
                return jo, oslot, ms_cs
            jj, slot, msf = ring_slot()
            w("sync", msf)
            ms = slot_dma(jj, lambda E, d=slot, s=dram_ap: E.dma_start(out=d, in_=s))
            return jj, slot, ms

        bfin_user = [None, None]

        def emit_ln(src, bfin_p, gv=None, bv=None, dstf=None, out_bf=True,
                    src_ready=None, extra_swait=None, extra_twait=None, extra_vwait=None,
                    twopass=False, apply_gp=False):
            BFIN = BFINS[bfin_p]
            per_chunk = isinstance(src_ready, list) and len(src_ready) == KC
            if not per_chunk:
                w("scalar", src_ready)
            w("scalar", extra_swait)
            w("scalar", bfin_user[bfin_p])
            cast_ms = [None] * KC
            sq_ms = [None] * KC
            mmS_ms = [None] * KC
            SQB = BFINS[1 - bfin_p] if twopass else BFIN
            sq_p = (1 - bfin_p) if twopass else bfin_p
            if twopass:
                w("scalar", bfin_user[sq_p])
            for k in range(KC):
                if per_chunk:
                    w("scalar", src_ready[k])
                cast_ms[k] = opi("scalar", "S", lambda E, k=k: E.activation(out=xc(BFIN, k), in_=src(k), func=AF.Copy))
            w("tensor", bank_free[0])
            w("tensor", extra_twait)
            w("tensor", cast_ms[KC - 1])
            for k in range(KC):
                mmS_ms[k] = opi("tensor", "T", lambda E, k=k: E.matmul(PB[0][:], ONESB[:], xc(BFIN, k), start=(k == 0), stop=(k == KC - 1)))
            for k in range(KC):
                if not twopass:
                    w("scalar", mmS_ms[k])
                sq_ms[k] = opi("scalar", "S", lambda E, k=k: E.activation(out=xc(SQB, k), in_=src(k), func=AF.Square))
            w("tensor", bank_free[1])
            mmQ_last = None
            for k in range(KC):
                w("tensor", sq_ms[k])
                mmQ_last = opi("tensor", "T", lambda E, k=k: E.matmul(PB[1][:], ONESB[:], xc(SQB, k), start=(k == 0), stop=(k == KC - 1)))
            bank_ready[0] = mmS_ms[KC - 1]
            bank_ready[1] = mmQ_last
            bfin_user[sq_p] = mmQ_last
            if twopass:
                bfin_user[bfin_p] = None
            w("vector", bank_ready[0])
            w("vector", [st.get("p2_done"), st.get("pl2_done")])
            opi("vector", "V", lambda E: E.tensor_scalar(out=MEANT[:], in0=PB[0][:], scalar1=float(1.0 / D), scalar2=None, op0=OP.mult))
            opi("vector", "V", lambda E: E.tensor_tensor(out=ZS[:, 0:T], in0=MEANT[:], in1=MEANT[:], op=OP.mult))
            w("vector", bank_ready[1])
            opi("vector", "V", lambda E: E.scalar_tensor_tensor(out=PB[1][:], in0=PB[1][:], scalar=float(1.0 / D), in1=ZS[:, 0:T], op0=OP.mult, op1=OP.subtract))
            ms_veps = opi("vector", "V", lambda E: E.tensor_scalar(out=PB[1][:], in0=PB[1][:], scalar1=float(EPS), scalar2=None, op0=OP.add))
            w("scalar", ms_veps)
            ms_sqr = opi("scalar", "S", lambda E: E.activation(out=PB[1][:], in_=PB[1][:], func=AF.Sqrt))
            w("vector", ms_sqr)
            ms_rec = opi("vector", "V", lambda E: E.reciprocal(out=RSTD[:], in_=PB[1][:]))
            bank_free[1] = ms_rec
            ms_pow = ms_rec
            bf_ms = [None] * KC
            f_ms = [None] * KC
            if twopass:
                st["p1_stats"] = ms_pow

                def pass1():
                    for k in range(KC):
                        opi("vector", "V", lambda E, k=k: E.tensor_tensor(out=ZS[:, 0:T], in0=src(k), in1=MEANT[:], op=OP.subtract))
                        bf_ms[k] = opi("vector", "V", lambda E, k=k: E.tensor_tensor(out=xc(BFIN, k), in0=ZS[:, 0:T], in1=RSTD[:], op=OP.mult))
                    bank_free[0] = bf_ms[KC - 1]
                    bank_free[1] = bf_ms[KC - 1]
                st["p1_pass1"] = pass1

                def p2_step(k):
                    def go():
                        if k == 0:
                            w("gpsimd", [extra_vwait, bf_ms[KC - 1]])
                        ms_sub = opi("gpsimd", "G", lambda E, k=k: E.tensor_tensor(out=dstf(k), in0=src(k), in1=MEANT[:], op=OP.subtract))
                        w("gpsimd", ms_sub)
                        f_ms[k] = opi("gpsimd", "G", lambda E, k=k: E.tensor_tensor(out=dstf(k), in0=dstf(k), in1=RSTD[:], op=OP.mult))
                        st["p2f_%d" % k] = f_ms[k]
                        if k == KC - 1:
                            st["p2_done"] = f_ms[k]
                    return go
                st["p1_pass2_steps"] = [p2_step(k) for k in range(KC)]
                st["p1_sq"] = sq_ms
                return cast_ms, None, sq_ms[KC - 1], bf_ms, f_ms
            if apply_gp:
                w("gpsimd", [ms_rec, extra_vwait])
                for k in range(KC):
                    m1 = opi("gpsimd", "G", lambda E, k=k: E.tensor_tensor(out=dstf(k), in0=src(k), in1=MEANT[:], op=OP.subtract))
                    w("gpsimd", m1)
                    f_ms[k] = opi("gpsimd", "G", lambda E, k=k: E.tensor_tensor(out=dstf(k), in0=dstf(k), in1=RSTD[:], op=OP.mult))
                st["pl2_done"] = f_ms[KC - 1]
                bank_free[0] = f_ms[KC - 1]
                bank_free[1] = f_ms[KC - 1]
                return None, f_ms[KC - 1], sq_ms[KC - 1], bf_ms, f_ms
            w("vector", extra_vwait)
            for k in range(KC):
                opi("vector", "V", lambda E, k=k: E.tensor_tensor(out=ZS[:, 0:T], in0=src(k), in1=MEANT[:], op=OP.subtract))
                if dstf is not None:
                    f_ms[k] = opi("vector", "V", lambda E, k=k: E.tensor_tensor(out=dstf(k), in0=ZS[:, 0:T], in1=RSTD[:], op=OP.mult))
                    if out_bf:
                        bf_ms[k] = opi("vector", "V", lambda E, k=k: E.tensor_copy(out=xc(BFIN, k), in_=dstf(k)))
                else:
                    opi("vector", "V", lambda E, k=k: E.tensor_tensor(out=ZS[:, 0:T], in0=ZS[:, 0:T], in1=RSTD[:], op=OP.mult))
                    bf_ms[k] = opi("vector", "V", lambda E, k=k: E.tensor_scalar(out=xc(BFIN, k), in0=ZS[:, 0:T], scalar1=gv[:, k:k + 1], scalar2=bv[:, k:k + 1], op0=OP.mult, op1=OP.add))
            last_bf = bf_ms[KC - 1]
            last_f = f_ms[KC - 1]
            mfin = last_f if (dstf is not None and not out_bf) else last_bf
            bank_free[0] = mfin
            bank_free[1] = mfin
            return last_bf, last_f, sq_ms[KC - 1], bf_ms, f_ms

        st = {}

        def make_chain(j):
            t0 = j * T
            BFIN = BFINS[j % 2]
            st.pop("ln1_bf", None)

            def spec_mms(fc, wait_s, wait_v, percol=None):
                w("tensor", st.get("b7free"))
                w("tensor", wait_s)
                w("tensor", wait_v)
                if j == 0:
                    w("tensor", ms_wto)
                mmm = None
                for k in range(KC):
                    if percol is not None:
                        w("tensor", percol[k])
                    mmm = opi("tensor", "T", lambda E, k=k, fc=fc: E.matmul(
                        PB[7][:], WTO[:, k * 512 + fc * 128: k * 512 + (fc + 1) * 128], xc(BFIN, k),
                        start=(k == 0), stop=(k == KC - 1)))
                return mmm

            def prep(ch, ms_ur_evac, ms_ui_mm):
                if j == 0:
                    w("vector", [ms_tab, ms_dfu])
                w("vector", [ms_ur_evac, ms_ui_mm])
                opi("vector", "V", lambda E, ch=ch: E.tensor_tensor(out=xc(VR, ch), in0=ZS[:, 0:T], in1=cfc(ch, 0, T), op=OP.mult))
                opi("vector", "V", lambda E, ch=ch: E.tensor_tensor(out=YR[:, 0:T], in0=PB[7][:], in1=sfc(ch, 0, T), op=OP.mult))
                opi("vector", "V", lambda E, ch=ch: E.tensor_tensor(out=xc(VR, ch), in0=xc(VR, ch), in1=YR[:, 0:T], op=OP.add))
                opi("vector", "V", lambda E, ch=ch: E.tensor_tensor(out=xc(VI, ch), in0=PB[7][:], in1=cfc(ch, 0, T), op=OP.mult))
                opi("vector", "V", lambda E, ch=ch: E.tensor_tensor(out=YI[:, 0:T], in0=ZS[:, 0:T], in1=sfc(ch, 0, T), op=OP.mult))
                return opi("vector", "V", lambda E, ch=ch: E.tensor_tensor(out=xc(VI, ch), in0=xc(VI, ch), in1=YI[:, 0:T], op=OP.subtract))

            steps = []

            def s_ln1a():
                w("scalar", x_ms[j])
                w("scalar", bfin_user[j % 2])
                cast_ms = [None] * KC
                sq_ms = [None] * KC
                mmS_ms = [None] * KC
                for k in range(KC):
                    cast_ms[k] = opi("scalar", "S", lambda E, k=k: E.activation(out=xc(BFIN, k), in_=xc(X0, k), func=AF.Copy))
                w("tensor", bank_free[0])
                if j == 0:
                    w("tensor", ms_bh)
                w("tensor", cast_ms[KC - 1])
                for k in range(KC):
                    mmS_ms[k] = opi("tensor", "T", lambda E, k=k: E.matmul(PB[0][:], ONESB[:], xc(BFIN, k), start=(k == 0), stop=(k == KC - 1)))
                for k in range(KC):
                    w("scalar", mmS_ms[k])
                    sq_ms[k] = opi("scalar", "S", lambda E, k=k: E.activation(out=xc(BFIN, k), in_=xc(X0, k), func=AF.Square))
                w("tensor", bank_free[1])
                mmQ_last = None
                for k in range(KC):
                    w("tensor", sq_ms[k])
                    mmQ_last = opi("tensor", "T", lambda E, k=k: E.matmul(PB[1][:], ONESB[:], xc(BFIN, k), start=(k == 0), stop=(k == KC - 1)))
                bank_ready[0] = mmS_ms[KC - 1]
                bank_ready[1] = mmQ_last
                bfin_user[j % 2] = mmQ_last
            steps.append(s_ln1a)

            def s_ln1b():
                w("vector", [bank_ready[0], st.get("p2_done"), st.get("pl2_done")])
                opi("vector", "V", lambda E: E.tensor_scalar(out=MEANT[:], in0=PB[0][:], scalar1=float(1.0 / D), scalar2=None, op0=OP.mult))
                opi("vector", "V", lambda E: E.tensor_tensor(out=ZS[:, 0:T], in0=MEANT[:], in1=MEANT[:], op=OP.mult))
                w("vector", bank_ready[1])
                w("vector", st.get("p2_done"))
                opi("vector", "V", lambda E: E.scalar_tensor_tensor(out=PB[1][:], in0=PB[1][:], scalar=float(1.0 / D), in1=ZS[:, 0:T], op0=OP.mult, op1=OP.subtract))
                ms_veps = opi("vector", "V", lambda E: E.tensor_scalar(out=PB[1][:], in0=PB[1][:], scalar1=float(EPS), scalar2=None, op0=OP.add))
                w("scalar", ms_veps)
                ms_sqr = opi("scalar", "S", lambda E: E.activation(out=PB[1][:], in_=PB[1][:], func=AF.Sqrt))
                w("vector", ms_sqr)
                ms_rec = opi("vector", "V", lambda E: E.reciprocal(out=RSTD[:], in_=PB[1][:]))
                bank_free[1] = ms_rec
                ms_pow = ms_rec
            steps.append(s_ln1b)

            def s_ln1c(ks):
                bf_ms = st.setdefault("ln1_bf", [None] * KC)
                for k in ks:
                    opi("vector", "V", lambda E, k=k: E.tensor_tensor(out=ZS[:, 0:T], in0=xc(X0, k), in1=MEANT[:], op=OP.subtract))
                    opi("vector", "V", lambda E, k=k: E.tensor_tensor(out=ZS[:, 0:T], in0=ZS[:, 0:T], in1=RSTD[:], op=OP.mult))
                    bf_ms[k] = opi("vector", "V", lambda E, k=k: E.tensor_scalar(out=xc(BFIN, k), in0=ZS[:, 0:T], scalar1=G1V[:, k:k + 1], scalar2=B1V[:, k:k + 1], op0=OP.mult, op1=OP.add))
                if ks[-1] == KC - 1:
                    st["ln1_apply"] = bf_ms[KC - 1]
                    bank_free[0] = bf_ms[KC - 1]
                    bank_free[1] = bf_ms[KC - 1]
            steps.append(lambda: s_ln1c((0, 1, 2)))
            steps.append(lambda: s_ln1c((3, 4, 5)))
            steps.append(lambda: s_ln1c((6, 7)))

            def s_spec0():
                mm0 = spec_mms(0, None, None, percol=st["ln1_bf"])
                w("scalar", [mm0, st["ln1_apply"]])
                st["ev0"] = opi("scalar", "S", lambda E: E.activation(out=ZS[:, 0:T], in_=PB[7][:], func=AF.Copy))
                st["b7free"] = st["ev0"]
            steps.append(s_spec0)

            def s_spec2():
                st["mm2"] = spec_mms(2, st["ev0"], None)
            steps.append(s_spec2)

            def s_prep0():
                st["prep0"] = prep(0, st["ev0"], st["mm2"])
                st["b7free"] = st["prep0"]
            steps.append(s_prep0)

            def s_spec1():
                mm1 = spec_mms(1, None, st["prep0"])
                w("scalar", [mm1, st["prep0"]])
                st["ev1"] = opi("scalar", "S", lambda E: E.activation(out=ZS[:, 0:T], in_=PB[7][:], func=AF.Copy))
                st["b7free"] = st["ev1"]
            steps.append(s_spec1)

            def s_spec3():
                st["mm3"] = spec_mms(3, st["ev1"], None)
            steps.append(s_spec3)

            def s_prep1():
                st["prep1"] = prep(1, st["ev1"], st["mm3"])
                st["b7free"] = st["prep1"]
            steps.append(s_prep1)

            def s_sp(ch):
                opi("vector", "V", lambda E, ch=ch: E.tensor_tensor_scan(out=YR[:, 0:T], data0=xc(DFU, ch), data1=xc(VR, ch), initial=STR[:, ch:ch + 1], op0=OP.mult, op1=OP.add))
                opi("vector", "V", lambda E, ch=ch: E.tensor_tensor_scan(out=YI[:, 0:T], data0=xc(DFU, ch), data1=xc(VI, ch), initial=STI[:, ch:ch + 1], op0=OP.mult, op1=OP.add))
                opi("vector", "V", lambda E, ch=ch: E.tensor_tensor(out=xc(VR, ch), in0=YR[:, 0:T], in1=cfc(ch, 1, T + 1), op=OP.mult))
                opi("vector", "V", lambda E, ch=ch: E.tensor_tensor(out=xc(VI, ch), in0=YI[:, 0:T], in1=sfc(ch, 1, T + 1), op=OP.mult))
                opi("vector", "V", lambda E, ch=ch: E.tensor_tensor(out=xc(VR, ch), in0=xc(VR, ch), in1=xc(VI, ch), op=OP.subtract))
                opi("vector", "V", lambda E, ch=ch: E.tensor_scalar(out=xc(ZB, ch), in0=xc(VR, ch), scalar1=10.0, scalar2=-10.0, op0=OP.min, op1=OP.max))
                opi("vector", "V", lambda E, ch=ch: E.tensor_scalar(out=STR[:, ch:ch + 1], in0=VR[:, ch * T + T - 1: ch * T + T], scalar1=10.0, scalar2=-10.0, op0=OP.min, op1=OP.max))
                opi("vector", "V", lambda E, ch=ch: E.tensor_tensor(out=xc(VI, ch), in0=YI[:, 0:T], in1=cfc(ch, 1, T + 1), op=OP.mult))
                opi("vector", "V", lambda E, ch=ch: E.tensor_tensor(out=YR[:, 0:T], in0=YR[:, 0:T], in1=sfc(ch, 1, T + 1), op=OP.mult))
                opi("vector", "V", lambda E, ch=ch: E.tensor_tensor(out=xc(VI, ch), in0=xc(VI, ch), in1=YR[:, 0:T], op=OP.add))
                opi("vector", "V", lambda E, ch=ch: E.tensor_scalar(out=xc(ZB, 2 + ch), in0=xc(VI, ch), scalar1=10.0, scalar2=-10.0, op0=OP.min, op1=OP.max))
                ms = opi("vector", "V", lambda E, ch=ch: E.tensor_scalar(out=STI[:, ch:ch + 1], in0=VI[:, ch * T + T - 1: ch * T + T], scalar1=10.0, scalar2=-10.0, op0=OP.min, op1=OP.max))
                st["post%d" % ch] = ms
            steps.append(lambda: s_sp(0))
            steps.append(lambda: s_sp(1))

            def s_wfrom(ocs):
                for oc in ocs:
                    w("tensor", [st.get("b7free"), st["post1"]])
                    if j == 0:
                        w("tensor", ms_wfr)
                    mmm = None
                    for c in range(4):
                        mmm = opi("tensor", "T", lambda E, c=c, oc=oc: E.matmul(
                            PB[7][:], WFR[:, c * 1024 + oc * 128: c * 1024 + (oc + 1) * 128], xc(ZB, c),
                            start=(c == 0), stop=(c == 3)))
                    w("vector", [mmm, st.get("x2_free"), st.get("p2f_%d" % oc)])
                    ms = opi("vector", "V", lambda E, oc=oc, t0=t0: E.scalar_tensor_tensor(
                        out=xc(X2, oc), in0=PB[7][:], scalar=0.1, in1=xc(X0, oc), op0=OP.mult, op1=OP.add))
                    st["b7free"] = ms
                    st["x2_%d" % oc] = ms
            for oc in range(KC):
                steps.append(lambda oc=oc: s_wfrom((oc,)))
            return steps

        def emit_ffn1(i, ms_ln3, ln3_bf):
            BFIN = BFINS[i % 2]
            g_ms = [None] * 32
            for fr in range(8):
                jj, slot, msl = emit_slab_load(scr_w1[fr], ("w1", fr))
                w("tensor", msl)
                last_in_slab = None
                for fc4 in range(4):
                    fc = fr * 4 + fc4
                    b = 2 + fc4
                    w("tensor", bank_free[b])
                    mmm = None
                    for k in range(KC):
                        if fr == 0 and fc4 == 0:
                            w("tensor", ln3_bf[k])
                        mmm = opi("tensor", "T", lambda E, k=k, fc4=fc4, b=b, slot=slot: E.matmul(
                            PB[b][:], slot[:, k * 512 + fc4 * 128: k * 512 + (fc4 + 1) * 128], xc(BFIN, k),
                            start=(k == 0), stop=(k == KC - 1)))
                    bank_ready[b] = mmm
                    last_in_slab = mmm
                    w("scalar", mmm)
                    g_ms[fc] = opi("scalar", "S", lambda E, b=b, fc=fc: E.activation(out=xc(G, fc), in_=PB[b][:], func=AF.Gelu, bias=BHV[:, fc:fc + 1]))
                    bank_free[b] = g_ms[fc]
                ring_set_free(jj, last_in_slab)
                if fr == 3 and pending_p2[0] is not None:
                    pending_p2[0]()
                    pending_p2[0] = None
            return g_ms

        def emit_ffn2(i, g_ms, u_ms):
            x3_ms = [None] * KC
            if i == 0:
                w("tensor", ms_b2c)
            for fr in range(2):
                for kr in range(4):
                    jj, slot, msl = emit_slab_load(scr_w2[fr * 4 + kr], ("w2", fr * 4 + kr))
                    w("tensor", msl)
                    for jx in range(8):
                        w("tensor", g_ms[kr * 8 + jx])
                    mm_last = None
                    for oc4 in range(4):
                        oc = fr * 4 + oc4
                        b = 2 + oc4
                        if kr == 0:
                            w("tensor", bank_free[b])
                            opi("tensor", "T", lambda E, b=b, oc=oc: E.matmul(
                                PB[b][:], B2R[0:1, oc * 128:(oc + 1) * 128], ONER[:], start=True, stop=False))
                        for jx in range(8):
                            mm_last = opi("tensor", "T", lambda E, jx=jx, kr=kr, oc4=oc4, b=b, slot=slot: E.matmul(
                                PB[b][:], slot[:, jx * 512 + oc4 * 128: jx * 512 + (oc4 + 1) * 128], xc(G, kr * 8 + jx),
                                start=False, stop=(kr == 3 and jx == 7)))
                        if kr == 3:
                            bank_ready[b] = mm_last
                    ring_set_free(jj, mm_last)
                for oc4 in range(4):
                    oc = fr * 4 + oc4
                    b = 2 + oc4
                    w("vector", bank_ready[b])
                    x3_ms[oc] = opi("vector", "V", lambda E, oc=oc, b=b: E.scalar_tensor_tensor(
                        out=xc(X2, oc), in0=PB[b][:], scalar=0.1, in1=xc(X2, oc), op0=OP.mult, op1=OP.add))
                    bank_free[b] = x3_ms[oc]
                    u_ms[oc] = opi("vector", "V", lambda E, oc=oc: E.tensor_tensor(
                        out=xc(X2, oc), in0=xc(X0, oc), in1=xc(X2, oc), op=OP.add))
            return x3_ms

        last_gu_mm = [None]
        prev_out_so = [None]
        pending_p2 = [None]

        for s in make_chain(0):
            s()

        for i in range(NT):
            t0 = i * T
            x2_last = st["x2_7"]
            ms_ln3, _, _, ln3_bf, _ = emit_ln(lambda k: xc(X2, k), i % 2, gv=G3V, bv=B3V, src_ready=x2_last)
            g_ms = emit_ffn1(i, ms_ln3, ln3_bf)
            u_ms = [None] * KC
            x3_ms = emit_ffn2(i, g_ms, u_ms)
            if i + 1 < NT:
                emit_xload(i + 1, extra=u_ms[KC - 1])
            ms_p1bf, ms_p1f, ms_p1sq, p1_bf, p1_f = emit_ln(lambda k: xc(X2, k), i % 2, dstf=lambda k: xc(ACC, k), out_bf=True,
                                                            src_ready=u_ms, extra_vwait=prev_out_so[0], twopass=True)
            st["x2_free"] = ms_p1sq
            BFIN = BFINS[i % 2]
            w("tensor", bank_free[6])
            if i == 0:
                w("tensor", [ms_rt, ms_srb, ms_rto])
            lg_last = None
            for tg in range(4):
                for k in range(KC):
                    if tg == 0:
                        w("tensor", ms_p1bf[k])
                    lg_last = opi("tensor", "T", lambda E, tg=tg, k=k, BFIN=BFIN: E.matmul(
                        PB[6][:, tg * 9:(tg + 1) * 9], xc(BFIN, k)[:, tg * 128:(tg + 1) * 128], RT[:, k * 9:(k + 1) * 9],
                        start=(k == 0), stop=(k == KC - 1)))
            SQB2 = BFINS[(i + 1) % 2]
            for tg in range(4):
                for k in range(KC):
                    if tg == 0:
                        w("tensor", st["p1_sq"][k])
                    lg_last = opi("tensor", "T", lambda E, tg=tg, k=k, SQB2=SQB2: E.matmul(
                        PB[6][:, 36 + tg:37 + tg], xc(SQB2, k)[:, tg * 128:(tg + 1) * 128],
                        RT[:, 8:9],
                        start=(k == 0), stop=(k == KC - 1)))
            P63 = PB[6][:, 0:36].rearrange("p (g c) -> p g c", g=4)
            w("vector", lg_last)
            opi("vector", "V", lambda E: E.tensor_scalar(out=CBE[:, 0:4], in0=P63[:, :, 8], scalar1=float(1.0 / D), scalar2=None, op0=OP.mult))
            opi("vector", "V", lambda E: E.tensor_tensor(out=CBE[:, 8:12], in0=CBE[:, 0:4], in1=CBE[:, 0:4], op=OP.mult))
            opi("vector", "V", lambda E: E.scalar_tensor_tensor(out=CBE[:, 4:8], in0=PB[6][:, 36:40], scalar=float(1.0 / D), in1=CBE[:, 8:12], op0=OP.mult, op1=OP.subtract))
            ms_veps2 = opi("vector", "V", lambda E: E.tensor_scalar(out=CBE[:, 4:8], in0=CBE[:, 4:8], scalar1=float(EPS), scalar2=None, op0=OP.add))
            w("scalar", ms_veps2)
            ms_sq2 = opi("scalar", "S", lambda E: E.activation(out=CBE[:, 4:8], in_=CBE[:, 4:8], func=AF.Sqrt))
            w("vector", ms_sq2)
            opi("vector", "V", lambda E: E.reciprocal(out=CBE[:, 4:8], in_=CBE[:, 4:8]))
            opi("vector", "V", lambda E: E.tensor_tensor(
                out=CBE[:, 8:40].rearrange("p (g c) -> p g c", g=4),
                in0=CBE[:, 0:4].unsqueeze(-1).broadcast_to((128, 4, 8)),
                in1=SRB[:].unsqueeze(1).broadcast_to((128, 4, 8)), op=OP.mult))
            opi("vector", "V", lambda E: E.tensor_tensor(out=LGS[:].rearrange("p (g c) -> p g c", g=4), in0=P63[:, :, 0:8], in1=CBE[:, 8:40].rearrange("p (g c) -> p g c", g=4), op=OP.subtract))
            ms_lgs = opi("vector", "V", lambda E: E.tensor_tensor(
                out=LGS[:].rearrange("p (g c) -> p g c", g=4),
                in0=LGS[:].rearrange("p (g c) -> p g c", g=4),
                in1=CBE[:, 4:8].unsqueeze(-1).broadcast_to((128, 4, 8)), op=OP.mult))
            bank_free[6] = ms_lgs
            for tg in range(4):
                op("vector", lambda E, tg=tg: E.max(out=MX4[:, tg * 8:(tg + 1) * 8], in_=LGS[:, tg * 8:(tg + 1) * 8]))
            MX3 = MX4.rearrange("p (g c) -> p g c", g=4)
            LG3 = LGS[:].rearrange("p (g c) -> p g c", g=4)
            CM3 = CMB[:].rearrange("p (g c) -> p g c", g=4)
            SC3 = ZS[:, 64:96].rearrange("p (g c) -> p g c", g=4)
            ms_d = opi("vector", "V", lambda E: E.tensor_tensor(out=WW4A, in0=MX3[:, :, 1], in1=MX3[:, :, 0], op=OP.subtract))
            w("scalar", ms_d)
            ms_w2 = opi("scalar", "S", lambda E: E.activation(out=WW4B, in_=WW4A, func=AF.Sigmoid))
            w("vector", ms_w2)
            opi("vector", "V", lambda E: E.tensor_scalar(out=WW4A, in0=WW4B, scalar1=-1.0, scalar2=1.0, op0=OP.mult, op1=OP.add))
            opi("vector", "V", lambda E: E.tensor_tensor(out=CM3, in0=LG3, in1=MX3[:, :, 0:1].broadcast_to((128, 4, 8)), op=OP.is_equal))
            opi("vector", "V", lambda E: E.tensor_tensor(out=CM3, in0=CM3, in1=WW4A.unsqueeze(-1).broadcast_to((128, 4, 8)), op=OP.mult))
            opi("vector", "V", lambda E: E.tensor_tensor(out=SC3, in0=LG3, in1=MX3[:, :, 1:2].broadcast_to((128, 4, 8)), op=OP.is_equal))
            opi("vector", "V", lambda E: E.tensor_tensor(out=SC3, in0=SC3, in1=WW4B.unsqueeze(-1).broadcast_to((128, 4, 8)), op=OP.mult))
            cmb_ms = opi("vector", "V", lambda E: E.tensor_tensor(out=CMB[:], in0=CMB[:], in1=ZS[:, 64:96], op=OP.add))
            w("tensor", [cmb_ms, ms_lgs, ms_sl if i == 0 else None])
            tr_last = None
            for tg in range(4):
                tr_last = opi("tensor", "T", lambda E, tg=tg: E.transpose(
                    PB[6][0:8, tg * 128:(tg + 1) * 128], CMB[:, tg * 8:(tg + 1) * 8], EYE[:]))
            w("scalar", tr_last)
            ms_cte = opi("scalar", "S", lambda E: E.activation(out=CTE[:], in_=PB[6][0:8, 0:T], func=AF.Copy))
            bank_free[6] = ms_cte
            st["p1_pass1"]()

            st["accinit"] = None

            bg = list(st.pop("p1_pass2_steps", []))
            if i + 1 < NT:
                bg += list(make_chain(i + 1))

            def bg_pop(n=1):
                for _ in range(n):
                    if bg:
                        bg.pop(0)()

            prev_upe = [None, None]
            prev_prd = [None] * 8
            prev_cbe_readers = ms_lgs
            acc_ms = p1_f
            for e in range(NE):
                w("tensor", [ms_cte if e == 0 else None, ms_eh if (i == 0 and e == 0) else None])
                w("tensor", bank_free[6])
                ms_cb = opi("tensor", "T", lambda E, e=e: E.matmul(
                    PB[6][:], EH[0:8, e * 128:(e + 1) * 128], CTE[:], start=True, stop=True))
                w("scalar", [ms_cb, prev_cbe_readers])
                ms_cbe = opi("scalar", "S", lambda E: E.activation(out=CBE[:], in_=PB[6][:], func=AF.Copy))
                bank_free[6] = ms_cbe
                gu_last = None
                sg_ms = [None] * 8
                last_up_tt = None
                for fr in (0, 2, 1, 3):
                    jj, slot, msl = emit_slab_load(scr_gu[e, fr], ("gu", e, fr))
                    w("tensor", msl)
                    gu_last_slab = None
                    for fc4 in range(4):
                        gc = fr * 4 + fc4
                        b = 2 + fc4
                        w("tensor", bank_free[b])
                        mmm = None
                        for k in range(KC):
                            if e == 0 and fr == 0 and fc4 == 0:
                                w("tensor", p1_bf[k])
                            mmm = opi("tensor", "T", lambda E, k=k, fc4=fc4, b=b, slot=slot, BFIN=BFIN: E.matmul(
                                PB[b][:], slot[:, k * 512 + fc4 * 128: k * 512 + (fc4 + 1) * 128], xc(BFIN, k),
                                start=(k == 0), stop=(k == KC - 1)))
                        bank_ready[b] = mmm
                        gu_last = gu_last_slab = mmm
                        if gc < 8:
                            w("scalar", [mmm, prev_prd[gc]])
                            sg_ms[gc] = opi("scalar", "S", lambda E, b=b, gc=gc: E.activation(out=xc(SG, gc), in_=PB[b][:], func=AF.Silu))
                            bank_free[b] = sg_ms[gc]
                        else:
                            uc = gc - 8
                            par = uc % 2
                            w("vector", [mmm, ms_cbe, prev_upe[par]])
                            ms_up = opi("vector", "V", lambda E, b=b, par=par: E.tensor_tensor(out=xc(UPE, par), in0=PB[b][:], in1=CBE[:], op=OP.mult))
                            bank_free[b] = ms_up
                            last_up_tt = ms_up
                            peng = "gpsimd" if (uc % 2 == 0) else "vector"
                            psem = "G" if (uc % 2 == 0) else "V"
                            w(peng, [ms_up, sg_ms[uc]])
                            prev_upe[par] = opi(peng, psem, lambda E, uc=uc, par=par: E.tensor_tensor(out=xc(PRD, uc), in0=xc(SG, uc), in1=xc(UPE, par), op=OP.mult))
                            prev_prd[uc] = prev_upe[par]
                    ring_set_free(jj, gu_last_slab)
                    bg_pop()
                last_gu_mm[0] = gu_last
                bfin_user[i % 2] = gu_last
                prev_cbe_readers = last_up_tt
                if e == 0:
                    bg_pop(4)
                for fr in range(2):
                    jj, slot, msl = emit_slab_load(scr_dp[e, fr], ("dp", e, fr))
                    w("tensor", msl)
                    if fr == 0:
                        w("tensor", [prev_prd[k] for k in range(8)])
                    dn_last_slab = None
                    for oc4 in range(4):
                        oc = fr * 4 + oc4
                        b = 2 + oc4
                        w("tensor", bank_free[b])
                        mmm = None
                        for k in range(KC):
                            mmm = opi("tensor", "T", lambda E, k=k, oc4=oc4, b=b, slot=slot: E.matmul(
                                PB[b][:], slot[:, k * 512 + oc4 * 128: k * 512 + (oc4 + 1) * 128], xc(PRD, k),
                                start=(k == 0), stop=(k == KC - 1)))
                        bank_ready[b] = mmm
                        dn_last_slab = mmm
                        w("vector", [mmm, acc_ms[oc]])
                        acc_ms[oc] = opi("vector", "V", lambda E, oc=oc, b=b: E.tensor_tensor(out=xc(ACC, oc), in0=PB[b][:], in1=xc(ACC, oc), op=OP.add))
                        bank_free[b] = acc_ms[oc]
                    ring_set_free(jj, dn_last_slab)
                    bg_pop()

            while bg:
                bg.pop(0)()

            def make_p2(i=i, t0=t0, acc_ms=list(acc_ms), gu_ms=last_gu_mm[0]):
                def p2():
                    _, ms_out, _, _, _ = emit_ln(lambda k: xc(ACC, k), i % 2, dstf=lambda k: xc(ACC, k), out_bf=False,
                                                 src_ready=acc_ms[KC - 1], extra_swait=gu_ms)
                    w("sync", ms_out)
                    for k in range(KC):
                        opd("sync", "SO", lambda E, k=k, t0=t0: E.dma_start(out=outTr[k, :, t0:t0 + T], in_=xc(ACC, k)))
                    prev_out_so[0] = ("SO", cnt["SO"])
                return p2
            pending_p2[0] = make_p2()
            if i == NT - 1:
                pending_p2[0]()
                pending_p2[0] = None

        for sk in ("SO", "SX", "SR", "R0", "R1", "R2", "R3"):
            if cnt[sk]:
                w("sync", (sk, cnt[sk]))

        block = es.enter_context(nc.Block())
        for eng_name, dec in (("sync", block.sync), ("tensor", block.tensor),
                              ("vector", block.vector), ("scalar", block.scalar),
                              ("gpsimd", block.gpsimd)):
            def body(E, eng_name=eng_name):
                for fn in q[eng_name]:
                    fn(E)
            dec(body)
    return nc


_CACHE = {}


def _get_nc(stop_stage):
    if stop_stage not in _CACHE:
        _CACHE[stop_stage] = build(stop_stage)
    return _CACHE[stop_stage]


def kernel(L0_attn_x, L0_attn_W_to, L0_attn_log_decay, L0_attn_frequencies,
           L0_attn_W_from, L0_attn_ln1_g, L0_attn_ln1_b, L0_attn_ln3_g,
           L0_attn_ln3_b, L0_attn_W_ffn1, L0_attn_b_ffn1, L0_attn_W_ffn2,
           L0_attn_b_ffn2, L0_ffn_router_weight, L0_ffn_gate_up_proj,
           L0_ffn_down_proj):
    x = np.asarray(L0_attn_x, dtype=np.float32)
    B = x.shape[0]
    C = np.ascontiguousarray
    shared = {
        "wtoT": C(np.asarray(L0_attn_W_to, np.float32).T),
        "wfromT": C(np.asarray(L0_attn_W_from, np.float32).T),
        "w1T": C(np.asarray(L0_attn_W_ffn1, np.float32).T),
        "w2T": C(np.asarray(L0_attn_W_ffn2, np.float32).T),
        "rtT": C(np.asarray(L0_ffn_router_weight, np.float32).T),
        "guT": C(np.asarray(L0_ffn_gate_up_proj, np.float32).transpose(0, 2, 1)),
        "dpT": C(np.asarray(L0_ffn_down_proj, np.float32).transpose(0, 2, 1)),
        "ln1g": C(np.asarray(L0_attn_ln1_g, np.float32)),
        "ln1b": C(np.asarray(L0_attn_ln1_b, np.float32)),
        "ln3g": C(np.asarray(L0_attn_ln3_g, np.float32)),
        "ln3b": C(np.asarray(L0_attn_ln3_b, np.float32)),
        "bf1": C(np.asarray(L0_attn_b_ffn1, np.float32)),
        "bf2": C(np.asarray(L0_attn_b_ffn2, np.float32)),
        "ldec": C(np.asarray(L0_attn_log_decay, np.float32)),
        "freq": C(np.asarray(L0_attn_frequencies, np.float32)),
        "eye_in": np.eye(128, dtype=np.float32),
    }
    in_maps = []
    for c in range(B):
        m = dict(shared)
        m["xT"] = C(x[c].T)
        in_maps.append(m)
    nc = _get_nc(STOP_STAGE)
    res = run_bass_kernel_spmd(nc, in_maps, core_ids=list(range(B)))
    out = np.stack([np.asarray(r["outT"]).T for r in res.results], axis=0)
    return out.astype(np.float32)
